# version 21
# baseline (speedup 1.0000x reference)
import math
import numpy as np
import ml_dtypes
import concourse.bass as bass
import concourse.mybir as mybir
from contextlib import ExitStack

F32 = mybir.dt.float32
BF16 = mybir.dt.bfloat16
F16 = mybir.dt.float16
U8 = mybir.dt.uint8
AF = mybir.ActivationFunctionType
ALU = mybir.AluOpType
AX = mybir.AxisListType

D = 2048
T = 2048
HD = 128
NH = 16
NKV = 4
I5 = 5632
NT = 4
TT = 512
DC = 16
HC = 44
INV_SQRT_HD = 1.0 / math.sqrt(HD)
CST = {4.0: 0, -2.0: 1, -6.0: 2, 1e-6: 3, 1e-5: 4}


class Ops:
    ENGS = ("sync", "tensor", "vector", "scalar", "gpsimd")

    def __init__(self, nc):
        self.nc = nc
        self.ops = {e: [] for e in self.ENGS}

    def emit(self, no_drain=True):
        with self.nc.Block(no_gpsimd_drain=no_drain) as blk:
            for e in self.ENGS:
                lst = self.ops[e]
                if lst:
                    getattr(blk, e)(lambda eng, _l=tuple(lst): [f(eng) for f in _l])


class Builder:
    def __init__(self, dump=()):
        self.dump = set(dump)
        self.nc = bass.Bass("TRN2", target_bir_lowering=False)
        nc = self.nc
        ap = lambda t: t.ap()
        self.x = ap(nc.dram_tensor("x", [T, D], F32, kind="ExternalInput"))
        self.WqT = ap(nc.dram_tensor("WqT", [D, 2048], F32, kind="ExternalInput"))
        self.WkT = ap(nc.dram_tensor("WkT", [D, 512], F32, kind="ExternalInput"))
        self.WvT = ap(nc.dram_tensor("WvT", [D, 512], F32, kind="ExternalInput"))
        self.WoT = ap(nc.dram_tensor("WoT", [2048, D], F32, kind="ExternalInput"))
        self.qs_row = ap(nc.dram_tensor("qs_row", [1, HD], F32, kind="ExternalInput"))
        self.ks_row = ap(nc.dram_tensor("ks_row", [1, HD], F32, kind="ExternalInput"))
        self.guB = ap(nc.dram_tensor("guB", [D, 2 * I5], U8, kind="ExternalInput"))
        self.guS = ap(nc.dram_tensor("guS", [D, 2 * I5], F16, kind="ExternalInput"))
        self.dnB = ap(nc.dram_tensor("dnB", [I5, D], U8, kind="ExternalInput"))
        self.dnS = ap(nc.dram_tensor("dnS", [I5, D], F16, kind="ExternalInput"))
        self.maskt = ap(nc.dram_tensor("maskt", [128, 128], F32, kind="ExternalInput"))
        self.ident = ap(nc.dram_tensor("ident", [128, 128], F32, kind="ExternalInput"))
        self.yout = ap(nc.dram_tensor("yout", [T, D], F32, kind="ExternalOutput"))
        k = lambda nm: "ExternalOutput" if nm in self.dump else "Internal"
        self.q_bf = ap(nc.dram_tensor("q_bf", [2048, T], BF16, kind=k("q_bf")))
        self.concat_bf = ap(nc.dram_tensor("concat_bf", [2048, T], BF16, kind=k("concat_bf")))
        self.wgu_bf = ap(nc.dram_tensor("wgu_bf", [D, 2 * I5], BF16, kind=k("wgu_bf")))
        self.wdn_bf = ap(nc.dram_tensor("wdn_bf", [I5, D], BF16, kind=k("wdn_bf")))
        if "p4" in self.dump:
            self.d_x1 = ap(nc.dram_tensor("d_x1", [128, 4 * 2048], F32, kind="ExternalOutput"))
            self.d_ln2fm = ap(nc.dram_tensor("d_ln2fm", [128, DC * 512], BF16, kind="ExternalOutput"))
            self.d_h = ap(nc.dram_tensor("d_h", [128, HC * 512], BF16, kind="ExternalOutput"))
            self.d_y = ap(nc.dram_tensor("d_y", [128, 4 * 2048], F32, kind="ExternalOutput"))
            self.d_wdns = ap(nc.dram_tensor("d_wdns", [128, 4 * 512], BF16, kind="ExternalOutput"))
        if "p1" in self.dump:
            self.d_stats = ap(nc.dram_tensor("d_stats", [128, 24], F32, kind="ExternalOutput"))
            self.d_ln1fm = ap(nc.dram_tensor("d_ln1fm", [128, DC * TT], BF16, kind="ExternalOutput"))
            self.d_kfm = ap(nc.dram_tensor("d_kfm", [128, NKV * T], BF16, kind="ExternalOutput"))
            self.d_vtm = ap(nc.dram_tensor("d_vtm", [128, 16 * 512], BF16, kind="ExternalOutput"))
            self.d_rstdq = ap(nc.dram_tensor("d_rstdq", [128, 256], F32, kind="ExternalOutput"))
            self.d_mvec = ap(nc.dram_tensor("d_mvec", [128, 1], F32, kind="ExternalOutput"))

        self.top = ExitStack()
        self._sh = {}
        self._sn = {}
        sb = self.sb_top = lambda nm, shp, dt: self.top.enter_context(nc.sbuf_tensor(nm, shp, dt))
        self.ident16 = sb("ident16", [128, 128], BF16)
        self.mask16 = sb("mask16", [128, 128], BF16)
        self.mvec = sb("mvec", [128, 1], F32)
        self.cst = sb("cst", [128, 8], F32)
        FQ = self.FQ = 1024
        self.dq_msk = sb("dq_msk", [128, 2], mybir.dt.int32)
        self.dq_csc = sb("dq_csc", [128, 3], F32)
        self.dq_b = [sb(f"dq_b{i}", [128, FQ], U8) for i in range(2)]
        self.dq_sc = [sb(f"dq_sc{i}", [128, FQ], F16) for i in range(2)]
        self.dq_r = [sb(f"dq_r{i}", [128, FQ], U8) for i in range(2)]
        self.dq_s = [sb(f"dq_s{i}", [128, FQ], U8) for i in range(2)]
        self.dq_rf = [sb(f"dq_rf{i}", [128, FQ], F32) for i in range(2)]
        self.dq_sg = [sb("dq_sg0", [128, FQ], F32)] * 2
        self.dq_r1 = [sb(f"dq_r1{i}", [128, FQ], F32) for i in range(2)]
        self.dq_r2 = [sb("dq_r20", [128, FQ], F32)] * 2
        self.dq_o = [sb(f"dq_o{i}", [128, FQ], BF16) for i in range(2)]
        self.attn_scope = ExitStack()
        asb = lambda nm, shp, dt: self.attn_scope.enter_context(nc.sbuf_tensor(nm, shp, dt))
        self.ident32 = asb("ident32", [128, 128], F32)
        self.rstdq = asb("rstdq", [128, 256], F32)
        self.K_fm = asb("K_fm", [128, NKV * T], BF16)
        self.V_tm = asb("V_tm", [128, 16 * 512], BF16)
        self.dq_jobs = []
        for c in range(DC):
            for j in range(2 * I5 // FQ):
                self.dq_jobs.append((self.guB, self.guS, self.wgu_bf, c * 128, j * FQ))
        for c in range(HC):
            for j in range(D // FQ):
                self.dq_jobs.append((self.dnB, self.dnS, self.wdn_bf, c * 128, j * FQ))
        self.dq_i = 0
        self.dq_w_next = 0
        self.dq_gu_jobs = DC * (2 * I5 // FQ)

    def sem(self, name):
        if name not in self._sh:
            self._sh[name] = self.top.enter_context(self.nc.semaphore(name))
            self._sn[name] = 0
        return self._sh[name]

    def sval(self, name):
        self.sem(name)
        return self._sn[name]

    def op(self, B, eng, fn, inc=None, k=1):
        if inc is not None:
            h = self.sem(inc)
            self._sn[inc] += k
            v = self._sn[inc]

            def g(e, fn=fn, h=h, k=k):
                fn(e).then_inc(h, k)
            B.ops[eng].append(g)
            return v
        B.ops[eng].append(lambda e, fn=fn: fn(e))
        return None

    def opf(self, B, eng, fn):
        v = self.op(B, eng, fn, inc="ch_" + eng)
        self.wt(B, eng, "ch_" + eng, v)

    def wt(self, B, eng, name, val):
        if val is None or val <= 0:
            return
        h = self.sem(name)
        B.ops[eng].append(lambda e, h=h, v=val: e.wait_ge(h, v))

    def emit_dequant(self, B, n, pool_heavy=False):
        FQ = self.FQ
        for _ in range(n):
            if self.dq_i >= len(self.dq_jobs):
                return
            j = self.dq_i
            self.dq_i += 1
            srcB, srcS, dst, r0, c0 = self.dq_jobs[j]
            bb, sc = self.dq_b[j % 2], self.dq_sc[j % 2]
            if j >= 2:
                self.wt(B, "sync", "dq_x", j - 1)
                self.wt(B, "sync", "dq_w", j - 1)
            self.op(B, "sync", lambda g, bb=bb, a=srcB[r0:r0 + 128, c0:c0 + FQ]:
                    g.dma_start(bb[:, :], a), inc="dq_gb", k=16)
            self.op(B, "sync", lambda g, sc=sc, a=srcS[r0:r0 + 128, c0:c0 + FQ]:
                    g.dma_start(sc[:, :], a), inc="dq_gs", k=16)
            if self.dq_w_next == j - 1:
                self._dq_w(B, j - 1)
            r, s = self.dq_r[j % 2], self.dq_s[j % 2]
            self.wt(B, "vector", "dq_gb", 16 * (j + 1))
            if j >= 2:
                self.wt(B, "vector", "dq_a", j - 1)
                self.wt(B, "vector", "dq_p", j - 1)
            self.op(B, "vector", lambda v, bb=bb, r=r: v.tensor_scalar(
                r[:, :], bb[:, :], self.dq_msk[:, 0:1], None, ALU.bitwise_and))
            self.op(B, "vector", lambda v, bb=bb, s=s: v.tensor_scalar(
                s[:, :], bb[:, :], self.dq_msk[:, 1:2], None, ALU.bitwise_and),
                    inc="dq_x")
            rf, sg, r1, r2 = self.dq_rf[j % 2], self.dq_sg[j % 2], self.dq_r1[j % 2], self.dq_r2[j % 2]
            self.wt(B, "scalar", "dq_x", j + 1)
            if j >= 1:
                self.wt(B, "scalar", "dq_p", j)
            self.op(B, "scalar", lambda a, r1=r1, r=r: a.activation(
                r1[:, :], r[:, :], AF.Relu, bias=self.cs(-2.0), scale=self.dq_csc[:, 0:1]))
            self.op(B, "scalar", lambda a, r2=r2, r=r: a.activation(
                r2[:, :], r[:, :], AF.Relu, bias=self.cs(-6.0), scale=self.dq_csc[:, 2:3]),
                    inc="dq_a")
            self.wt(B, "gpsimd", "dq_x", j + 1)
            self.op(B, "gpsimd", lambda g, sg=sg, s=s: g.tensor_scalar(
                sg[:, :], s[:, :], self.dq_csc[:, 1:2], 1.0, ALU.mult, ALU.add))
            self.op(B, "gpsimd", lambda g, rf=rf, r=r: g.tensor_scalar(
                rf[:, :], r[:, :], self.dq_csc[:, 0:1], None, ALU.mult))
            self.wt(B, "gpsimd", "dq_a", j + 1)
            self.op(B, "gpsimd", lambda g, rf=rf, r1=r1: g.tensor_tensor(rf[:, :], rf[:, :], r1[:, :], ALU.add))
            self.op(B, "gpsimd", lambda g, rf=rf, r2=r2: g.tensor_tensor(rf[:, :], rf[:, :], r2[:, :], ALU.add))
            self.op(B, "gpsimd", lambda g, rf=rf, sg=sg: g.tensor_tensor(rf[:, :], rf[:, :], sg[:, :], ALU.mult),
                    inc="dq_p")

    def _dq_w(self, B, j):
        FQ = self.FQ
        srcB, srcS, dst, r0, c0 = self.dq_jobs[j]
        sc, oo = self.dq_sc[j % 2], self.dq_o[j % 2]
        self.wt(B, "gpsimd", "dq_gs", 16 * (j + 1))
        if j >= 2:
            self.wt(B, "gpsimd", "dq_o", 16 * (j - 1))
        self.op(B, "gpsimd", lambda g, sc=sc, oo=oo, rf=self.dq_rf[j % 2]: g.tensor_tensor(
            oo[:, :], rf[:, :], sc[:, :], ALU.mult), inc="dq_w")
        self.wt(B, "sync", "dq_w", j + 1)
        self.op(B, "sync", lambda g, oo=oo, a=dst[r0:r0 + 128, c0:c0 + FQ]: g.dma_start(a, oo[:, :]),
                inc="dq_o", k=16)
        self.dq_w_next = j + 1

    def dq_flush(self, B):
        if self.dq_w_next == self.dq_i - 1 and self.dq_i >= 1:
            self._dq_w(B, self.dq_i - 1)

    def cs(self, val):
        return self.cst[:, CST[val]:CST[val] + 1]

    def p0(self):
        B = Ops(self.nc)
        for val, i in CST.items():
            self.op(B, "vector", lambda v, val=val, i=i: v.memset(self.cst[:, i:i + 1], val))
        self.op(B, "vector", lambda v: v.memset(self.dq_msk[0:64, 0:1], 7))
        self.op(B, "vector", lambda v: v.memset(self.dq_msk[64:128, 0:1], 112))
        self.op(B, "vector", lambda v: v.memset(self.dq_msk[0:64, 1:2], 8))
        self.op(B, "vector", lambda v: v.memset(self.dq_msk[64:128, 1:2], 128))
        self.op(B, "vector", lambda v: v.memset(self.dq_csc[0:64, 0:1], 0.5))
        self.op(B, "vector", lambda v: v.memset(self.dq_csc[64:128, 0:1], 1.0 / 32.0))
        self.op(B, "vector", lambda v: v.memset(self.dq_csc[0:64, 1:2], -0.25))
        self.op(B, "vector", lambda v: v.memset(self.dq_csc[64:128, 1:2], -1.0 / 64.0))
        self.op(B, "vector", lambda v: v.memset(self.dq_csc[0:64, 2:3], 1.0))
        self.op(B, "vector", lambda v: v.memset(self.dq_csc[64:128, 2:3], 1.0 / 16.0))
        self.op(B, "gpsimd", lambda g: g.dma_start(self.ident16[:, :], self.ident), inc="dgp", k=16)
        self.op(B, "sync", lambda s: s.dma_start(self.ident32[:, :], self.ident), inc="dsy", k=16)
        self.op(B, "gpsimd", lambda g: g.dma_start(self.mask16[:, :], self.maskt), inc="dgp", k=16)
        asb2 = lambda nm, shp, dt: self.attn_scope.enter_context(self.nc.sbuf_tensor(nm, shp, dt))
        qsr = self.qs32 = asb2("qs32", [128, 128], F32)
        ksr = self.ks32 = asb2("ks32", [128, 128], F32)
        self.op(B, "vector", lambda v: v.memset(self.rstdq[:, :], 0.0))
        self.op(B, "vector", lambda v: v.memset(qsr[:, :], 0.0))
        self.op(B, "vector", lambda v: v.memset(ksr[:, :], 0.0), inc="p0m")
        self.wt(B, "sync", "p0m", self.sval("p0m"))
        self.op(B, "sync", lambda s: s.dma_start(qsr[0:1, :], self.qs_row), inc="dsy", k=16)
        self.op(B, "sync", lambda s: s.dma_start(ksr[0:1, :], self.ks_row), inc="dsy", k=16)
        B.emit()
        B = Ops(self.nc)
        with self.nc.psum_tensor("ps_qk", [128, 512], F32) as ps_qk:
            self.wt(B, "tensor", "dsy", self.sval("dsy"))
            self.op(B, "tensor", lambda t: t.transpose(ps_qk[:, 0:128], self.qs32[:, :], self.ident32[:, :]))
            self.op(B, "tensor", lambda t: t.transpose(ps_qk[:, 128:256], self.ks32[:, :], self.ident32[:, :]),
                    inc="p0t")
            self.wt(B, "vector", "p0t", self.sval("p0t"))
            self.opf(B, "vector", lambda v: v.tensor_copy(self.K_fm[:, 0:256], ps_qk[:, 0:256]))
            self.opf(B, "vector", lambda v: v.tensor_tensor(self.mvec[:, :], self.K_fm[:, 0:1], self.K_fm[:, 128:129], ALU.mult))
            self.op(B, "vector", lambda v: v.tensor_scalar(self.mvec[:, :], self.mvec[:, :], INV_SQRT_HD, None, ALU.mult))
            B.emit()

    def p1(self, parts="ab", ntiles=NT):
        nc = self.nc
        with ExitStack() as st:
            sb = lambda nm, shp, dt: st.enter_context(nc.sbuf_tensor(nm, shp, dt))
            bufs = dict(
                x_sb=sb("x_sb", [128, 2 * 2048], F32),
                ln1tm=sb("ln1tm", [128, 2 * 2048], BF16),
                ln1fm=sb("ln1fm", [128, DC * TT], BF16),
                rows=sb("rows", [128, 8], F32),
                m_sb=sb("m_sb", [128, 4], F32),
                t_a=sb("t_a", [128, 4], F32),
                t_b=sb("t_b", [128, 4], F32),
                rs_sb=sb("rs_sb", [128, 4], F32),
                wq_sb=[sb(f"wq_sb{i}", [128, DC * 512], BF16) for i in range(2)],
                wk_sb=sb("wk_sb", [128, DC * 512], BF16),
                wv_sb=sb("wv_sb", [128, DC * 512], BF16),
                ktm=sb("ktm", [128, 512], BF16),
                ksq=sb("ksq", [128, 8], F32),
                qtm=sb("qtm", [128, 4 * 512], BF16),
                qstage=[sb(f"qstage{i}", [128, 512], BF16) for i in range(2)],
            )
            for ti in range(ntiles):
                self._p1_tile(ti, bufs, parts)
            B = Ops(nc)
            self.op(B, "scalar", lambda a: a.activation(self.rstdq[:, :], self.rstdq[:, :], AF.Sqrt,
                                                        bias=self.cs(1e-6), scale=1.0 / HD), inc="p1f")
            self.wt(B, "vector", "p1f", self.sval("p1f"))
            self.op(B, "vector", lambda v: v.reciprocal(self.rstdq[:, :], self.rstdq[:, :]))
            if "p1" in self.dump:
                self.op(B, "sync", lambda s: s.dma_start(self.d_kfm, self.K_fm[:, :]), inc="dsy", k=16)
                self.op(B, "sync", lambda s: s.dma_start(self.d_vtm, self.V_tm[:, :]), inc="dsy", k=16)
            B.emit(no_drain=False)
            if "p1" in self.dump:
                B = Ops(nc)
                self.op(B, "sync", lambda s: s.dma_start(self.d_rstdq, self.rstdq[:, :]), inc="dsy", k=16)
                self.op(B, "sync", lambda s: s.dma_start(self.d_mvec, self.mvec[:, :]), inc="dsy", k=16)
                B.emit(no_drain=False)

    def _p1_tile(self, ti, bufs, parts="ab"):
        self._bufs_dbg = bufs
        nc = self.nc
        S = self.sval
        x_sb, ln1tm, ln1fm = bufs["x_sb"], bufs["ln1tm"], bufs["ln1fm"]
        rows, m_sb, t_a, t_b, rs_sb = bufs["rows"], bufs["m_sb"], bufs["t_a"], bufs["t_b"], bufs["rs_sb"]
        wq_sb, wk_sb, wv_sb = bufs["wq_sb"], bufs["wk_sb"], bufs["wv_sb"]
        ktm, ksq, qtm, qstage = bufs["ktm"], bufs["ksq"], bufs["qtm"], bufs["qstage"]

        def lnfm_dst(s, half):
            v = ln1fm[:, :].rearrange("p (dc f) -> p dc f", f=512)
            return v[:, half * 8:(half + 1) * 8, s * 128:(s + 1) * 128]

        B = Ops(nc)
        if ti == 0:
            self.op(B, "gpsimd", lambda g: g.dma_start(
                wk_sb[:, :], self.WkT.rearrange("(c p) f -> p c f", p=128)), inc="dgkv", k=16)
            self.op(B, "gpsimd", lambda g: g.dma_start(
                wv_sb[:, :], self.WvT.rearrange("(c p) f -> p c f", p=128)), inc="dgkv", k=16)
        for hg in range(2):
            self.op(B, "gpsimd", lambda g, hg=hg: g.dma_start(
                wq_sb[hg][:, :],
                self.WqT[:, hg * 512:(hg + 1) * 512].rearrange("(c p) f -> p c f", p=128)),
                inc="dgq", k=16)
        st_done = {}
        if ti > 0:
            self.wt(B, "vector", "p1st", self.sval("p1st"))
            self.wt(B, "scalar", "p1st", self.sval("p1st"))
        self.op(B, "vector", lambda v: v.memset(rows[:, 0:4], 0.0), inc="p1rm")
        self.op(B, "scalar", lambda a: a.memzero(rows[:, 4:8]), inc="p1rm")
        for s in range(4):
            if s == 0:
                self.wt(B, "vector", "p1rm", self.sval("p1rm"))
                self.wt(B, "scalar", "p1rm", self.sval("p1rm"))
            if s >= 2:
                self.wt(B, "sync", "p1st", st_done[s - 2])
            xd = self.op(B, "sync", lambda y, s=s: y.dma_start(
                x_sb[:, (s % 2) * 2048:(s % 2 + 1) * 2048],
                self.x[ti * TT + s * 128: ti * TT + (s + 1) * 128, :]), inc="dx", k=16)
            xsl = x_sb[:, (s % 2) * 2048:(s % 2 + 1) * 2048]
            self.wt(B, "vector", "dx", xd)
            self.op(B, "vector", lambda v, xsl=xsl, s=s: v.tensor_scalar(
                ln1tm[:, 0:2048], xsl, 1.0, 0.0, ALU.mult, ALU.add, accum_out=rows[:, s:s + 1]),
                inc="p1st")
            self.wt(B, "scalar", "dx", xd)
            self.op(B, "scalar", lambda a, xsl=xsl, s=s: a.activation(
                ln1tm[:, 2048:4096], xsl, AF.Square, accum_out=rows[:, 4 + s:5 + s]),
                inc="p1st")
            st_done[s] = S("p1st")
        self.emit_dequant(B, 2)
        self.wt(B, "vector", "p1st", st_done[3])
        self.opf(B, "vector", lambda v: v.tensor_scalar(m_sb[:, :], rows[:, 0:4], 1.0 / D, None, ALU.mult))
        self.opf(B, "vector", lambda v: v.tensor_scalar(t_a[:, :], rows[:, 4:8], 1.0 / D, None, ALU.mult))
        self.opf(B, "vector", lambda v: v.tensor_tensor(t_b[:, :], m_sb[:, :], m_sb[:, :], ALU.mult))
        self.op(B, "vector", lambda v: v.tensor_tensor(t_a[:, :], t_a[:, :], t_b[:, :], ALU.subtract),
                inc="p1v")
        self.wt(B, "scalar", "p1v", S("p1v"))
        self.op(B, "scalar", lambda a: a.activation(t_b[:, :], t_a[:, :], AF.Sqrt, bias=self.cs(1e-5), scale=1.0),
                inc="p1sd")
        self.wt(B, "vector", "p1sd", S("p1sd"))
        self.opf(B, "vector", lambda v: v.reciprocal(rs_sb[:, :], t_b[:, :]))
        with nc.psum_tensor(f"ps_tr0_{ti}", [128, 1024], BF16) as pt0, \
             nc.psum_tensor(f"ps_tr1_{ti}", [128, 1024], BF16) as pt1:
            ps_tr = [pt0, pt1]
            ln_done = {}
            te_done = {}
            for s in range(4):
                if s < 2:
                    self.wt(B, "sync", "p1st", st_done[3])
                else:
                    self.wt(B, "sync", "p1ln", ln_done[s - 2])
                xd = self.op(B, "sync", lambda y, s=s: y.dma_start(
                    x_sb[:, (s % 2) * 2048:(s % 2 + 1) * 2048],
                    self.x[ti * TT + s * 128: ti * TT + (s + 1) * 128, :]), inc="dx", k=16)
                xsl = x_sb[:, (s % 2) * 2048:(s % 2 + 1) * 2048]
                self.wt(B, "vector", "dx", xd)
                if s >= 2:
                    self.wt(B, "vector", "p1tr", S("p1tr"))
                self.op(B, "vector", lambda v, xsl=xsl, s=s: v.tensor_scalar(
                    ln1tm[:, (s % 2) * 2048:(s % 2 + 1) * 2048], xsl,
                    m_sb[:, s:s + 1], rs_sb[:, s:s + 1], ALU.subtract, ALU.mult),
                    inc="p1ln")
                ln_done[s] = S("p1ln")
                for half in range(2):
                    u = 2 * s + half
                    self.wt(B, "tensor", "p1ln", ln_done[s])
                    if u >= 2:
                        self.wt(B, "tensor", "p1te", te_done[u - 2])
                    for i8 in range(8):
                        dc = half * 8 + i8
                        self.op(B, "tensor", lambda t, s=s, dc=dc, i8=i8, u=u: t.transpose(
                            ps_tr[u % 2][:, i8 * 128:(i8 + 1) * 128],
                            ln1tm[:, (s % 2) * 2048 + dc * 128: (s % 2) * 2048 + (dc + 1) * 128],
                            self.ident16[:, :]), inc="p1tr" if i8 == 7 else None)
                    self.wt(B, "vector", "p1tr", S("p1tr"))
                    self.op(B, "vector", lambda v, s=s, half=half, u=u: v.tensor_copy(
                        lnfm_dst(s, half),
                        ps_tr[u % 2][:, :].rearrange("p (dc f) -> p dc f", f=128)),
                        inc="p1te")
                    te_done[u] = S("p1te")
            B.emit()

        if not (set(parts) & set("bkq")):
            return
        do_kv = ("b" in parts) or ("k" in parts)
        do_q = ("b" in parts) or ("q" in parts)
        B = Ops(nc)
        self.emit_dequant(B, 7)
        if ti == 0 and "p1" in self.dump:
            self.op(B, "sync", lambda s: s.dma_start(self.d_ln1fm, ln1fm[:, :]), inc="dsy", k=16)
            self.op(B, "sync", lambda s: s.dma_start(self.d_stats[:, 0:8], rows[:, :]), inc="dsy", k=16)
            self.op(B, "sync", lambda s: s.dma_start(self.d_stats[:, 8:12], m_sb[:, :]), inc="dsy", k=16)
            self.op(B, "sync", lambda s: s.dma_start(self.d_stats[:, 12:16], rs_sb[:, :]), inc="dsy", k=16)
            self.op(B, "sync", lambda s: s.dma_start(self.d_stats[:, 16:20], bufs["t_a"][:, :]), inc="dsy", k=16)
            self.op(B, "sync", lambda s: s.dma_start(self.d_stats[:, 20:24], bufs["t_b"][:, :]), inc="dsy", k=16)
        with ExitStack() as st:
            psum = lambda nm: st.enter_context(nc.psum_tensor(f"{nm}_{ti}", [128, 512], F32))
            psum16 = lambda nm: st.enter_context(nc.psum_tensor(f"{nm}_{ti}", [128, 1024], BF16))
            ps_k = [psum("ps_k0"), psum("ps_k1")]
            ps_v = [psum("ps_v0"), psum("ps_v1")]
            ps_ktr = psum16("ps_ktr")
            ps_q = [psum("ps_q0"), psum("ps_q1")]
            ps_qtr = psum16("ps_qtr")
            if not do_kv:
                kv_iter = []
            else:
                kv_iter = list(range(4))
            if do_kv and ti == 0:
                self.wt(B, "tensor", "dgkv", S("dgkv"))
            kv_done = {}
            for tc in kv_iter:
                tcg = ti * 4 + tc
                if tc >= 2:
                    kn, ke = kv_done[tc - 2]
                    self.wt(B, "tensor", "p1kn", kn)
                    self.wt(B, "tensor", "p1ve", ke)
                for dc in range(DC):
                    lh = ln1fm[:, dc * 512 + tc * 128: dc * 512 + (tc + 1) * 128]
                    self.op(B, "tensor", lambda t, lh=lh, tc=tc, dc=dc: t.matmul(
                        ps_k[tc % 2][:, :], lh, wk_sb[:, dc * 512:(dc + 1) * 512],
                        start=(dc == 0), stop=(dc == DC - 1)))
                    self.op(B, "tensor", lambda t, lh=lh, tc=tc, dc=dc: t.matmul(
                        ps_v[tc % 2][:, :], lh, wv_sb[:, dc * 512:(dc + 1) * 512],
                        start=(dc == 0), stop=(dc == DC - 1)),
                        inc="p1kv" if dc == DC - 1 else None)
                self.wt(B, "vector", "p1kv", S("p1kv"))
                self.op(B, "vector", lambda v, tc=tc, tcg=tcg: v.tensor_copy(
                    self.V_tm[:, tcg * 512:(tcg + 1) * 512], ps_v[tc % 2][:, :]), inc="p1ve")
                self.wt(B, "scalar", "p1kv", S("p1kv"))
                self.op(B, "scalar", lambda a: a.memzero(ksq[:, 0:4]))
                for g in range(4):
                    self.op(B, "scalar", lambda a, tc=tc, g=g: a.activation(
                        ktm[:, 0:128], ps_k[tc % 2][:, g * 128:(g + 1) * 128],
                        AF.Square, accum_out=ksq[:, g:g + 1]))
                self.opf(B, "scalar", lambda a: a.nop())
                self.op(B, "scalar", lambda a: a.activation(
                    ksq[:, 4:8], ksq[:, 0:4], AF.Sqrt, bias=self.cs(1e-6), scale=1.0 / HD), inc="p1ks")
                self.wt(B, "vector", "p1ks", S("p1ks"))
                self.opf(B, "vector", lambda v: v.reciprocal(ksq[:, 4:8], ksq[:, 4:8]))
                for g in range(4):
                    self.op(B, "vector", lambda v, tc=tc, g=g: v.tensor_scalar(
                        ktm[:, g * 128:(g + 1) * 128], ps_k[tc % 2][:, g * 128:(g + 1) * 128],
                        ksq[:, 4 + g:5 + g], None, ALU.mult),
                        inc="p1kn" if g == 3 else None)
                self.wt(B, "tensor", "p1kn", S("p1kn"))
                if tc >= 1:
                    self.wt(B, "tensor", "p1kte", S("p1kte"))
                for g in range(4):
                    self.op(B, "tensor", lambda t, g=g: t.transpose(
                        ps_ktr[:, g * 128:(g + 1) * 128], ktm[:, g * 128:(g + 1) * 128],
                        self.ident16[:, :]), inc="p1kt" if g == 3 else None)
                self.wt(B, "vector", "p1kt", S("p1kt"))
                for g in range(4):
                    self.op(B, "vector", lambda v, g=g, tcg=tcg: v.tensor_copy(
                        self.K_fm[:, g * 2048 + tcg * 128: g * 2048 + (tcg + 1) * 128],
                        ps_ktr[:, g * 128:(g + 1) * 128]),
                        inc="p1kte" if g == 3 else None)
                kv_done[tc] = (S("p1kn"), S("p1ve"))
            if not do_q:
                B.emit()
                return
            qg_done = {}
            qt_done = {}
            qdma_done = {}
            self._q_mm_after = {}
            nhg = 4
            for ch in parts:
                if ch.isdigit():
                    nhg = int(ch)
            for hg in range(nhg):
                if hg >= 2:
                    self.wt(B, "gpsimd", "p1q_mm", self._q_mm_after[(hg - 2) * 4 + 3])
                    self.op(B, "gpsimd", lambda g, hg=hg: g.dma_start(
                        wq_sb[hg % 2][:, :],
                        self.WqT[:, hg * 512:(hg + 1) * 512].rearrange("(c p) f -> p c f", p=128)),
                        inc="dgq", k=16)
                self.wt(B, "tensor", "dgq", 16 * (4 * ti + hg + 1))
                for s in range(4):
                    gq = hg * 4 + s
                    if gq >= 2:
                        ev, sq = qg_done[gq - 2]
                        self.wt(B, "tensor", "p1q_ev", ev)
                        self.wt(B, "tensor", "p1q_sq", sq)
                    if s == 0 and hg >= 1:
                        self.wt(B, "tensor", "p1qt", qt_done[hg - 1])
                    for dc in range(DC):
                        self.op(B, "tensor", lambda t, hg=hg, s=s, dc=dc: t.matmul(
                            ps_q[s % 2][:, :],
                            ln1fm[:, dc * 512 + s * 128: dc * 512 + (s + 1) * 128],
                            wq_sb[hg % 2][:, dc * 512:(dc + 1) * 512],
                            start=(dc == 0), stop=(dc == DC - 1)),
                            inc="p1q_mm" if dc == DC - 1 else None)
                    self._q_mm_after[gq] = S("p1q_mm")
                    self.wt(B, "vector", "p1q_mm", S("p1q_mm"))
                    self.op(B, "vector", lambda v, s=s: v.tensor_copy(
                        qtm[:, s * 512:(s + 1) * 512], ps_q[s % 2][:, :]), inc="p1q_ev")
                    if "z" not in parts:
                        self.wt(B, "scalar", "p1q_ev", S("p1q_ev"))
                        for hh in range(4):
                            col = (hg * 4 + hh) * 16 + ti * 4 + s
                            tgt = bufs["ksq"][:, 0:1] if "w" in parts else self.rstdq[:, col:col + 1]
                            if "u" in parts:
                                self.op(B, "scalar", lambda a, s=s, hh=hh: a.activation(
                                    ktm[:, 0:128], ps_q[s % 2][:, hh * 128:(hh + 1) * 128],
                                    AF.Square),
                                    inc="p1q_sq" if hh == 3 else None)
                            else:
                                self.op(B, "scalar", lambda a, s=s, hh=hh, tgt=tgt: a.activation(
                                    ktm[:, 0:128], ps_q[s % 2][:, hh * 128:(hh + 1) * 128],
                                    AF.Square, accum_out=tgt),
                                    inc="p1q_sq" if hh == 3 else None)
                    else:
                        self.op(B, "scalar", lambda a: a.nop(), inc="p1q_sq")
                    qg_done[gq] = (S("p1q_ev"), S("p1q_sq"))
                if "y" in parts:
                    qt_done[hg] = S("p1qt")
                    continue
                self.wt(B, "tensor", "p1q_ev", qg_done[hg * 4 + 3][0])
                for hh in range(4):
                    h = hg * 4 + hh
                    if h >= 1:
                        self.wt(B, "tensor", "p1qte", S("p1qte"))
                    for s in range(4):
                        self.op(B, "tensor", lambda t, hh=hh, s=s: t.transpose(
                            ps_qtr[:, s * 128:(s + 1) * 128],
                            qtm[:, s * 512 + hh * 128: s * 512 + (hh + 1) * 128],
                            self.ident16[:, :]), inc="p1qt" if s == 3 else None)
                    self.wt(B, "vector", "p1qt", S("p1qt"))
                    if h >= 2:
                        self.wt(B, "vector", "dqb", qdma_done[h - 2])
                    self.op(B, "vector", lambda v, h=h: v.tensor_scalar(
                        qstage[h % 2][:, :], ps_qtr[:, 0:512],
                        self.mvec[:, :], None, ALU.mult), inc="p1qte")
                    self.wt(B, "sync", "p1qte", S("p1qte"))
                    qdma_done[h] = self.op(B, "sync", lambda y, h=h: y.dma_start(
                        self.q_bf[h * 128:(h + 1) * 128, ti * TT:(ti + 1) * TT],
                        qstage[h % 2][:, :]), inc="dqb", k=16)
                qt_done[hg] = S("p1qt")
                self.emit_dequant(B, 2)
            B.emit()


    def p3(self):
        nc = self.nc
        S = self.sval
        with ExitStack() as st:
            sb = lambda nm, shp, dt: st.enter_context(nc.sbuf_tensor(nm, shp, dt))
            qh_sb = [sb(f"qh{i}", [128, 2048], BF16) for i in range(2)]
            scores = [sb(f"sc{i}", [128, 2048], F32) for i in range(2)]
            probs = [sb(f"pr{i}", [128, 2048], BF16) for i in range(4)]
            diag = [sb(f"dg{i}", [128, 128], BF16) for i in range(4)]
            pT = [sb(f"pT{i}", [128, 512], BF16) for i in range(2)]
            outh = [sb(f"oh{i}", [128, 512], BF16) for i in range(2)]
            nmx = sb("nmx", [128, 4], F32)
            nb = sb("nb", [128, 4], F32)
            rsum = sb("rsum", [128, 8], F32)
            ps_sc = [st.enter_context(nc.psum_tensor(f"ps_sc{i}", [128, 512], F32)) for i in range(2)]
            ps_pT = [st.enter_context(nc.psum_tensor(f"ps_pT{i}", [128, 512], F32)) for i in range(2)]
            ps_oh = [st.enter_context(nc.psum_tensor(f"ps_oh{i}", [128, 512], F32)) for i in range(2)]
            self._ps_oh = ps_oh
            self._pT = pT
            qh_mm_done = {}
            st_done = {}
            dmm_done = {}
            pte_done = {}
            oe_done = {}
            gidx = 0
            tglob = 0
            sqglob = 0
            B = Ops(nc)
            qh_dma = {}
            self.wt(B, "sync", "dqb", self.sval("dqb"))
            qh_dma[0] = self.op(B, "sync", lambda y: y.dma_start(
                qh_sb[0][:, :], self.q_bf[0:128, :]), inc="dqh", k=16)
            for h in range(NH):
                g = h // 4
                if h + 1 < NH:
                    if h >= 1:
                        self.wt(B, "sync", "p3mm", qh_mm_done[h - 1])
                    qh_dma[h + 1] = self.op(B, "sync", lambda y, h1=h + 1: y.dma_start(
                        qh_sb[h1 % 2][:, :], self.q_bf[h1 * 128:(h1 + 1) * 128, :]), inc="dqh", k=16)
                self.wt(B, "tensor", "dqh", qh_dma[h])
                for sq in range(4):
                    for qc4 in range(4):
                        qc = 4 * sq + qc4
                        kw = (qc + 1) * 128
                        ngrp = (kw + 511) // 512
                        qcg = h * 16 + qc
                        if qcg >= 2:
                            self.wt(B, "vector", "p3ex", self._ex_done.get(qcg - 2, 0))
                        for g0 in range(ngrp):
                            N = min(512, kw - g0 * 512)
                            last = (g0 == ngrp - 1)
                            if gidx >= 2:
                                self.wt(B, "tensor", "p3st", st_done[gidx - 2])
                            self.op(B, "tensor", lambda t, h=h, qc=qc, g=g, g0=g0, N=N, gi=gidx, last=last: t.matmul(
                                ps_sc[gi % 2][:, 0:N], qh_sb[h % 2][:, qc * 128:(qc + 1) * 128],
                                self.K_fm[:, g * 2048 + g0 * 512: g * 2048 + g0 * 512 + N],
                                start=True, stop=not last),
                                inc=None if last else "p3mm")
                            if last:
                                self.op(B, "tensor", lambda t, N=N, gi=gidx: t.matmul(
                                    ps_sc[gi % 2][:, N - 128:N], self.ident16[:, :],
                                    self.mask16[:, :], start=False, stop=True), inc="p3mm")
                            self.wt(B, "scalar", "p3mm", S("p3mm"))
                            self.op(B, "scalar", lambda a, qc=qc, g0=g0, N=N, gi=gidx: a.activation(
                                scores[qc % 2][:, g0 * 512:g0 * 512 + N], ps_sc[gi % 2][:, 0:N],
                                AF.Copy), inc="p3st")
                            st_done[gidx] = S("p3st")
                            gidx += 1
                        self.wt(B, "vector", "p3st", st_done[gidx - 1])
                        self.opf(B, "vector", lambda v, qc=qc, kw=kw, qc4=qc4: v.reduce_max(
                            nmx[:, qc4:qc4 + 1], scores[qc % 2][:, 0:kw], axis=AX.X, negate=True))
                        self.op(B, "vector", lambda v, h=h, qc=qc, qc4=qc4: v.tensor_tensor(
                            nb[:, qc4:qc4 + 1], nmx[:, qc4:qc4 + 1],
                            self.rstdq[:, h * 16 + qc:h * 16 + qc + 1], ALU.mult), inc="p3nb")
                        self.wt(B, "scalar", "p3nb", S("p3nb"))
                        if (sqglob - 1, qc4) in dmm_done:
                            self.wt(B, "scalar", "p3dmm", dmm_done[(sqglob - 1, qc4)])
                        self.op(B, "scalar", lambda a, h=h, qc=qc, qc4=qc4, kw=kw: a.activation(
                            probs[qc4][:, 0:kw], scores[qc % 2][:, 0:kw], AF.Exp,
                            bias=nb[:, qc4:qc4 + 1],
                            scale=self.rstdq[:, h * 16 + qc:h * 16 + qc + 1],
                            accum_out=rsum[:, qc4:qc4 + 1]), inc="p3ex")
                        self._ex_done = getattr(self, "_ex_done", {})
                        self._ex_done[h * 16 + qc] = S("p3ex")
                        self.wt(B, "vector", "p3ex", S("p3ex"))
                        self.opf(B, "vector", lambda v, qc4=qc4: v.reciprocal(
                            rsum[:, 4 + qc4:5 + qc4], rsum[:, qc4:qc4 + 1]))
                        self.op(B, "vector", lambda v, qc4=qc4: v.tensor_scalar(
                            diag[qc4][:, :], self.ident16[:, :], rsum[:, 4 + qc4:5 + qc4],
                            None, ALU.mult), inc="p3dg")
                    dg_now = S("p3dg")
                    self.emit_dequant(B, 1)
                    ntk = 4 * (sq + 1)
                    for tkc in range(ntk):
                        nz = [qc4 for qc4 in range(4) if tkc <= 4 * sq + qc4]
                        zr = [qc4 for qc4 in range(4) if tkc > 4 * sq + qc4]
                        if tglob >= 2:
                            self.wt(B, "vector", "p3dmm", dmm_done.get(("pt", tglob - 2), 0))
                        if zr:
                            for qc4 in zr:
                                self.op(B, "vector", lambda v, qc4=qc4, ti_=tglob: v.memset(
                                    ps_pT[ti_ % 2][:, qc4 * 128:(qc4 + 1) * 128], 0.0),
                                    inc="p3pz" if qc4 == zr[-1] else None)
                            self.wt(B, "tensor", "p3pz", S("p3pz"))
                        self.wt(B, "tensor", "p3dg", dg_now)
                        if tglob >= 2:
                            self.wt(B, "tensor", "p3pte", pte_done[tglob - 2])
                        for i, qc4 in enumerate(nz):
                            self.op(B, "tensor", lambda t, qc4=qc4, tkc=tkc, ti_=tglob: t.matmul(
                                ps_pT[ti_ % 2][:, qc4 * 128:(qc4 + 1) * 128],
                                probs[qc4][:, tkc * 128:(tkc + 1) * 128], diag[qc4][:, :],
                                start=True, stop=True),
                                inc="p3dmm" if i == len(nz) - 1 else None)
                        for qc4 in nz:
                            dmm_done[(sqglob, qc4)] = S("p3dmm")
                        dmm_done[("pt", tglob)] = S("p3dmm")
                        self.wt(B, "vector", "p3dmm", S("p3dmm"))
                        self.op(B, "vector", lambda v, ti_=tglob, tkc=tkc: v.tensor_copy(
                            pT[ti_ % 2][:, :], ps_pT[ti_ % 2][:, :]), inc="p3pte")
                        pte_done[tglob] = S("p3pte")
                        if tkc >= 1:
                            self._p3_av(B, g, sq, tkc - 1, tglob - 1, ntk, sqglob, pte_done)
                        if tkc == ntk - 1:
                            self._p3_av(B, g, sq, tkc, tglob, ntk, sqglob, pte_done)
                        tglob += 1
                    self.wt(B, "vector", "p3av", S("p3av"))
                    if sqglob >= 2:
                        self.wt(B, "vector", "dcn", oe_done.get(sqglob - 2, 0))
                    self._oh_evac = getattr(self, "_oh_evac", {})
                    self.op(B, "vector", lambda v, sq=sq: v.tensor_copy(
                        outh[sq % 2][:, :], ps_oh[sq % 2][:, :]), inc="p3oe")
                    self._oh_evac[sqglob] = S("p3oe")
                    self.wt(B, "sync", "p3oe", S("p3oe"))
                    oe_done[sqglob] = self.op(B, "sync", lambda y, h=h, sq=sq: y.dma_start(
                        self.concat_bf[h * 128:(h + 1) * 128, sq * 512:(sq + 1) * 512],
                        outh[sq % 2][:, :]), inc="dcn", k=16)
                    sqglob += 1
                qh_mm_done[h] = S("p3mm")
            B.emit()


    def _p3_av(self, B, g, sq, tkc, tg, ntk, sqglob, pte_done):
        S = self.sval
        self.wt(B, "tensor", "p3pte", pte_done[tg])
        if tkc == 0 and sqglob >= 2:
            self.wt(B, "tensor", "p3oe", self._oh_evac.get(sqglob - 2, 0))
        self.op(B, "tensor", lambda t, g=g, tkc=tkc, ti_=tg, last=(tkc == ntk - 1), sq=sq: t.matmul(
            self._ps_oh[sq % 2][:, :], self.V_tm[:, tkc * 512 + g * 128: tkc * 512 + (g + 1) * 128],
            self._pT[ti_ % 2][:, :], start=(tkc == 0), stop=last),
            inc="p3av" if tkc == ntk - 1 else None)

    def p4(self):
        nc = self.nc
        S = self.sval
        for ti in range(NT):
            with ExitStack() as tst:
                tsb = lambda nm, shp, dt: tst.enter_context(nc.sbuf_tensor(f"{nm}_{ti}", shp, dt))
                x1 = tsb("x1", [128, 4 * 2048], F32)
                ln2fm = tsb("ln2fm", [128, DC * 512], BF16)
                self._p4a(ti, tst, x1, ln2fm)
                self._p4b(ti, tst, x1, ln2fm)

    def _p4a(self, ti, tst, x1, ln2fm):
        nc = self.nc
        S = self.sval
        with ExitStack() as st:
            sb = lambda nm, shp, dt: st.enter_context(nc.sbuf_tensor(f"{nm}_{ti}", shp, dt))
            concat_sb = sb("cc", [128, 16 * 512], BF16)
            wo_sb = [sb(f"wo{i}", [128, 16 * 512], BF16) for i in range(2)]
            x_sb = sb("x4", [128, 4 * 2048], F32)
            ln2tm = sb("ln2tm", [128, 2 * 2048], BF16)
            rows = sb("rws", [128, 8], F32)
            m_sb = sb("m4", [128, 4], F32)
            t_a = sb("ta4", [128, 4], F32)
            t_b = sb("tb4", [128, 4], F32)
            rs_sb = sb("rs4", [128, 4], F32)
            ps_wo = [st.enter_context(nc.psum_tensor(f"ps_wo{i}_{ti}", [128, 512], F32)) for i in range(2)]
            B = Ops(nc)
            if ti == 0:
                self.emit_dequant(B, 44)
            if ti == 0:
                self.wt(B, "sync", "dcn", self.sval("dcn"))
            cd = self.op(B, "sync", lambda y: y.dma_start(
                concat_sb[:, :],
                self.concat_bf[:, ti * TT:(ti + 1) * TT].rearrange("(c p) t -> p c t", p=128)),
                inc="dcc", k=16)
            xds = []
            for s in range(4):
                xds.append(self.op(B, "sync", lambda y, s=s: y.dma_start(
                    x_sb[:, s * 2048:(s + 1) * 2048],
                    self.x[ti * TT + s * 128: ti * TT + (s + 1) * 128, :]), inc="dx2", k=16))
            godone = []
            for dr in range(2):
                godone.append(self.op(B, "gpsimd", lambda g, dr=dr: g.dma_start(
                    wo_sb[dr][:, :],
                    self.WoT[:, dr * 512:(dr + 1) * 512].rearrange("(c p) d -> p c d", p=128)),
                    inc="dgo", k=16))
            gidx = 0
            ev_done = {}
            if ti >= 1:
                self.wt(B, "vector", "dout", self._out_dma_done.get(ti - 1, 0))
            for dr in range(4):
                if dr >= 2:
                    self.wt(B, "gpsimd", "p4wm", self._p4wm[dr - 2])
                    godone.append(self.op(B, "gpsimd", lambda g, dr=dr: g.dma_start(
                        wo_sb[dr % 2][:, :],
                        self.WoT[:, dr * 512:(dr + 1) * 512].rearrange("(c p) d -> p c d", p=128)),
                        inc="dgo", k=16))
                self.wt(B, "tensor", "dgo", godone[dr])
                if dr == 0:
                    self.wt(B, "tensor", "dcc", cd)
                for s in range(4):
                    if gidx >= 2:
                        self.wt(B, "tensor", "p4ev", ev_done[gidx - 2])
                    for cf in range(DC):
                        self.op(B, "tensor", lambda t, cf=cf, s=s, dr=dr, gi=gidx: t.matmul(
                            ps_wo[gi % 2][:, :],
                            concat_sb[:, cf * 512 + s * 128: cf * 512 + (s + 1) * 128],
                            wo_sb[dr % 2][:, cf * 512:(cf + 1) * 512],
                            start=(cf == 0), stop=(cf == DC - 1)),
                            inc="p4wm" if cf == DC - 1 else None)
                    self._p4wm = getattr(self, "_p4wm", {})
                    self._p4wm[dr] = S("p4wm")
                    self.wt(B, "vector", "p4wm", S("p4wm"))
                    self.wt(B, "vector", "dx2", xds[s])
                    self.op(B, "vector", lambda v, s=s, dr=dr, gi=gidx: v.tensor_tensor(
                        x1[:, s * 2048 + dr * 512: s * 2048 + (dr + 1) * 512],
                        ps_wo[gi % 2][:, :],
                        x_sb[:, s * 2048 + dr * 512: s * 2048 + (dr + 1) * 512], ALU.add),
                        inc="p4ev")
                    ev_done[gidx] = S("p4ev")
                    gidx += 1
            self.op(B, "vector", lambda v: v.memset(rows[:, 0:4], 0.0), inc="p4rm")
            self.op(B, "scalar", lambda a: a.memzero(rows[:, 4:8]), inc="p4rm")
            self.wt(B, "vector", "p4ev", S("p4ev"))
            self.wt(B, "scalar", "p4ev", S("p4ev"))
            self.wt(B, "scalar", "p4rm", S("p4rm"))
            self.wt(B, "vector", "p4rm", S("p4rm"))
            for s in range(4):
                xsl = x1[:, s * 2048:(s + 1) * 2048]
                self.op(B, "vector", lambda v, xsl=xsl, s=s: v.tensor_scalar(
                    ln2tm[:, 0:2048], xsl, 1.0, 0.0, ALU.mult, ALU.add, accum_out=rows[:, s:s + 1]),
                    inc="p4st")
                self.op(B, "scalar", lambda a, xsl=xsl, s=s: a.activation(
                    ln2tm[:, 2048:4096], xsl, AF.Square, accum_out=rows[:, 4 + s:5 + s]),
                    inc="p4st")
            self.wt(B, "vector", "p4st", S("p4st"))
            self.opf(B, "vector", lambda v: v.tensor_scalar(m_sb[:, :], rows[:, 0:4], 1.0 / D, None, ALU.mult))
            self.opf(B, "vector", lambda v: v.tensor_scalar(t_a[:, :], rows[:, 4:8], 1.0 / D, None, ALU.mult))
            self.opf(B, "vector", lambda v: v.tensor_tensor(t_b[:, :], m_sb[:, :], m_sb[:, :], ALU.mult))
            self.op(B, "vector", lambda v: v.tensor_tensor(t_a[:, :], t_a[:, :], t_b[:, :], ALU.subtract),
                    inc="p4v")
            self.wt(B, "scalar", "p4v", S("p4v"))
            self.op(B, "scalar", lambda a: a.activation(t_b[:, :], t_a[:, :], AF.Sqrt, bias=self.cs(1e-5), scale=1.0),
                    inc="p4sd")
            self.wt(B, "vector", "p4sd", S("p4sd"))
            self.opf(B, "vector", lambda v: v.reciprocal(rs_sb[:, :], t_b[:, :]))
            with nc.psum_tensor(f"ps_tr4a_{ti}", [128, 1024], BF16) as pt0, \
                 nc.psum_tensor(f"ps_tr4b_{ti}", [128, 1024], BF16) as pt1:
                ps_tr = [pt0, pt1]
                te_done = {}
                for s in range(4):
                    if s >= 2:
                        self.wt(B, "vector", "p4tr", S("p4tr"))
                    self.op(B, "vector", lambda v, s=s: v.tensor_scalar(
                        ln2tm[:, (s % 2) * 2048:(s % 2 + 1) * 2048], x1[:, s * 2048:(s + 1) * 2048],
                        m_sb[:, s:s + 1], rs_sb[:, s:s + 1], ALU.subtract, ALU.mult),
                        inc="p4ln")
                    lnd = S("p4ln")
                    for half in range(2):
                        u = 2 * s + half
                        self.wt(B, "tensor", "p4ln", lnd)
                        if u >= 2:
                            self.wt(B, "tensor", "p4te", te_done[u - 2])
                        for i8 in range(8):
                            dc = half * 8 + i8
                            self.op(B, "tensor", lambda t, s=s, dc=dc, i8=i8, u=u: t.transpose(
                                ps_tr[u % 2][:, i8 * 128:(i8 + 1) * 128],
                                ln2tm[:, (s % 2) * 2048 + dc * 128: (s % 2) * 2048 + (dc + 1) * 128],
                                self.ident16[:, :]), inc="p4trr" if i8 == 7 else None)
                        self.wt(B, "vector", "p4trr", S("p4trr"))
                        fmv = ln2fm[:, :].rearrange("p (dc f) -> p dc f", f=512)
                        dst = fmv[:, half * 8:(half + 1) * 8, s * 128:(s + 1) * 128]
                        self.op(B, "vector", lambda v, dst=dst, u=u: v.tensor_copy(
                            dst, ps_tr[u % 2][:, :].rearrange("p (dc f) -> p dc f", f=128)),
                            inc="p4te")
                        te_done[u] = S("p4te")
                        self.op(B, "vector", lambda v: v.nop(), inc="p4tr")
                if ti == 0 and "p4" in self.dump:
                    self.wt(B, "sync", "p4te", S("p4te"))
                    self.op(B, "sync", lambda y: y.dma_start(self.d_x1, x1[:, :]), inc="dsy", k=16)
                    self.op(B, "sync", lambda y: y.dma_start(self.d_ln2fm, ln2fm[:, :]), inc="dsy", k=16)
                B.emit()

    def _p4b(self, ti, tst, x1, ln2fm):
        nc = self.nc
        S = self.sval
        with ExitStack() as st:
            sb = lambda nm, shp, dt: st.enter_context(nc.sbuf_tensor(f"{nm}_{ti}", shp, dt))
            wgu_sb = [sb(f"wgu{i}", [128, 16 * 512], BF16) for i in range(2)]
            wdn_sb = [sb(f"wdn{i}", [128, 4 * 512], BF16) for i in range(2)]
            h_sb = sb("hh", [128, HC * 512], BF16)
            y_sb = sb("yy", [128, 4 * 2048], F32)
            silu_sb = [sb(f"sl{i}", [128, 512], F32) for i in range(2)]

            ysq = sb("ysq", [128, 24], F32)
            ps_g = [st.enter_context(nc.psum_tensor(f"ps_g{i}_{ti}", [128, 512], F32)) for i in range(2)]
            ps_u = [st.enter_context(nc.psum_tensor(f"ps_u{i}_{ti}", [128, 512], F32)) for i in range(2)]
            ps_y = [st.enter_context(nc.psum_tensor(f"ps_y{i}_{ti}", [128, 512], F32)) for i in range(4)]
            B = Ops(nc)
            if ti == 0:
                self.wt(B, "sync", "dq_o", 16 * self.dq_gu_jobs)
            gu_done = []
            for orange in range(2):
                gu_done.append(self.op(B, "sync", lambda y, orange=orange: y.dma_start(
                    wgu_sb[orange][:, :],
                    self.wgu_bf[:, orange * 512:(orange + 1) * 512].rearrange("(c p) o -> p c o", p=128)),
                    inc="dgu", k=16))
            sl_done = {}
            h_done = {}
            for orange in range(22):
                if orange >= 2:
                    self.wt(B, "sync", "p4gm", self._p4gm[(orange - 2) * 2 + 1])
                    gu_done.append(self.op(B, "sync", lambda y, orange=orange: y.dma_start(
                        wgu_sb[orange % 2][:, :],
                        self.wgu_bf[:, orange * 512:(orange + 1) * 512].rearrange("(c p) o -> p c o", p=128)),
                        inc="dgu", k=16))
                self.wt(B, "tensor", "dgu", gu_done[orange])
                if ti == 0:
                    self.emit_dequant(B, 2)
                for pidx in range(2):
                    pp = orange * 2 + pidx
                    j0 = 2 * pidx
                    if pp >= 2:
                        self.wt(B, "tensor", "p4sl", sl_done[pp - 2])
                        self.wt(B, "tensor", "p4h", h_done[pp - 2])
                    for i in range(DC):
                        self.op(B, "tensor", lambda t, i=i, j0=j0, pp=pp, orange=orange: t.matmul(
                            ps_g[pp % 2][:, :],
                            wgu_sb[orange % 2][:, i * 512 + j0 * 128: i * 512 + (j0 + 1) * 128],
                            ln2fm[:, i * 512:(i + 1) * 512],
                            start=(i == 0), stop=(i == DC - 1)))
                        self.op(B, "tensor", lambda t, i=i, j0=j0, pp=pp, orange=orange: t.matmul(
                            ps_u[pp % 2][:, :],
                            wgu_sb[orange % 2][:, i * 512 + (j0 + 1) * 128: i * 512 + (j0 + 2) * 128],
                            ln2fm[:, i * 512:(i + 1) * 512],
                            start=(i == 0), stop=(i == DC - 1)),
                            inc="p4gm" if i == DC - 1 else None)
                    self._p4gm = getattr(self, "_p4gm", {})
                    self._p4gm[pp] = S("p4gm")
                    self.wt(B, "scalar", "p4gm", S("p4gm"))
                    if pp >= 2:
                        self.wt(B, "scalar", "p4h", h_done[pp - 2])
                    self.op(B, "scalar", lambda a, pp=pp: a.activation(
                        silu_sb[pp % 2][:, :], ps_g[pp % 2][:, :], AF.Silu), inc="p4sl")
                    sl_done[pp] = S("p4sl")
                    self.wt(B, "vector", "p4sl", sl_done[pp])
                    self.op(B, "vector", lambda v, pp=pp: v.tensor_tensor(
                        h_sb[:, pp * 512:(pp + 1) * 512], silu_sb[pp % 2][:, :],
                        ps_u[pp % 2][:, :], ALU.mult), inc="p4h")
                    h_done[pp] = S("p4h")
            if ti == 0:
                self.wt(B, "sync", "dq_o", 16 * len(self.dq_jobs))
            dn_done = {}
            ye_done = {}
            didx = 0
            for dr in range(4):
                for iq in range(11):
                    if didx >= 2:
                        self.wt(B, "sync", "p4dm", dn_done.get(didx - 2, 0))
                    dnd = self.op(B, "sync", lambda y, dr=dr, iq=iq, di=didx: y.dma_start(
                        wdn_sb[di % 2][:, :],
                        self.wdn_bf[iq * 512:(iq + 1) * 512, dr * 512:(dr + 1) * 512]
                            .rearrange("(c p) d -> p c d", p=128)),
                        inc="dgd", k=16)
                    self.wt(B, "tensor", "dgd", dnd)
                    if ti == 0 and dr == 0 and iq == 0 and "p4" in self.dump:
                        self.wt(B, "sync", "dgd", dnd)
                        self.op(B, "sync", lambda y: y.dma_start(self.d_wdns, wdn_sb[0][:, :]), inc="dsy", k=16)
                    self.wt(B, "tensor", "p4h", h_done[iq * 4 + 3])
                    if iq == 0:
                        for tsub in range(4):
                            if dr >= 1:
                                self.wt(B, "tensor", "p4ye", ye_done[(dr - 1, tsub)])
                    for i4 in range(4):
                        ip = iq * 4 + i4
                        for tsub in range(4):
                            self.op(B, "tensor", lambda t, ip=ip, i4=i4, tsub=tsub, di=didx: t.matmul(
                                ps_y[tsub][:, :],
                                h_sb[:, ip * 512 + tsub * 128: ip * 512 + (tsub + 1) * 128],
                                wdn_sb[di % 2][:, i4 * 512:(i4 + 1) * 512],
                                start=(ip == 0), stop=(ip == HC - 1)),
                                inc="p4dm" if (i4 == 3 and tsub == 3) else None)
                    dn_done[didx] = S("p4dm")
                    didx += 1
                self.wt(B, "vector", "p4dm", S("p4dm"))
                for tsub in range(4):
                    self.op(B, "vector", lambda v, tsub=tsub, dr=dr: v.tensor_copy(
                        y_sb[:, tsub * 2048 + dr * 512: tsub * 2048 + (dr + 1) * 512],
                        ps_y[tsub][:, :]), inc="p4ye")
                    ye_done[(dr, tsub)] = S("p4ye")
                    self.wt(B, "scalar", "p4ye", S("p4ye"))
                    self.op(B, "scalar", lambda a, tsub=tsub, dr=dr: a.activation(
                        ln2fm[:, 0:512],
                        y_sb[:, tsub * 2048 + dr * 512: tsub * 2048 + (dr + 1) * 512],
                        AF.Square, accum_out=ysq[:, tsub * 4 + dr:tsub * 4 + dr + 1]),
                        inc="p4ys")
            self.wt(B, "vector", "p4ys", S("p4ys"))
            for tsub in range(4):
                self.opf(B, "vector", lambda v, tsub=tsub: v.tensor_reduce(
                    ysq[:, 16 + tsub:17 + tsub], ysq[:, tsub * 4:tsub * 4 + 4],
                    AX.X, ALU.add))
            self.op(B, "vector", lambda v: v.nop(), inc="p4yr")
            self.wt(B, "scalar", "p4yr", S("p4yr"))
            self.op(B, "scalar", lambda a: a.activation(
                ysq[:, 20:24], ysq[:, 16:20], AF.Sqrt, bias=self.cs(1e-6), scale=1.0 / D),
                inc="p4rm2")
            self.wt(B, "vector", "p4rm2", S("p4rm2"))
            self.opf(B, "vector", lambda v: v.reciprocal(ysq[:, 20:24], ysq[:, 20:24]))
            self._out_dma_done = getattr(self, "_out_dma_done", {})
            for tsub in range(4):
                self.op(B, "vector", lambda v, tsub=tsub: v.scalar_tensor_tensor(
                    x1[:, tsub * 2048:(tsub + 1) * 2048], y_sb[:, tsub * 2048:(tsub + 1) * 2048],
                    ysq[:, 20 + tsub:21 + tsub], x1[:, tsub * 2048:(tsub + 1) * 2048],
                    ALU.mult, ALU.add), inc="p4o")
                self.wt(B, "sync", "p4o", S("p4o"))
                self.op(B, "sync", lambda y, tsub=tsub: y.dma_start(
                    self.yout[ti * TT + tsub * 128: ti * TT + (tsub + 1) * 128, :],
                    x1[:, tsub * 2048:(tsub + 1) * 2048]), inc="dout", k=16)
            self._out_dma_done[ti] = S("dout")
            if ti == 0 and "p4" in self.dump:
                self.op(B, "sync", lambda y: y.dma_start(self.d_h, h_sb[:, :]), inc="dsy", k=16)
                self.op(B, "sync", lambda y: y.dma_start(self.d_y, y_sb[:, :]), inc="dsy", k=16)
            B.emit()


def build(dump=(), stages="full", ndq=None):
    b = Builder(dump)
    if ndq is not None:
        b.dq_jobs = b.dq_jobs[:ndq]
    b.p0()
    if stages in ("p1", "full"):
        b.p1()
    if stages in ("p3", "full"):
        if stages == "p3":
            b.p1()
        b.p3()
    if stages == "full":
        B = Ops(b.nc)
        while b.dq_i < len(b.dq_jobs):
            b.emit_dequant(B, 24)
        b.dq_flush(B)
        B.emit()
    b.attn_scope.close()
    if stages == "full":
        b.p4()
    elif stages.startswith("p1:"):
        _, parts, nt = stages.split(":")
        b.p1(parts, int(nt))
    if stages in ("dq", "full"):
        while b.dq_i < len(b.dq_jobs):
            B = Ops(b.nc)
            b.emit_dequant(B, 24)
            b.dq_flush(B)
            B.emit()
    B = Ops(b.nc)
    b.dq_flush(B)
    if b.dq_i:
        b.wt(B, "sync", "dq_o", 16 * b.dq_i)
    for s_ in ("dsy", "dx", "dqb", "dq_gb", "dq_gs", "dqh", "dcn",
               "dcc", "dx2", "dgo", "dgu", "dgd", "dout"):
        b.wt(B, "sync", s_, b.sval(s_))
    B.emit(no_drain=False)
    b.top.close()
    return b


_po = np.empty(128, np.int64)
_po[:64] = 2 * np.arange(64)
_po[64:] = 2 * np.arange(64) + 1
PERM_D = (128 * np.arange(16)[:, None] + _po[None, :]).ravel()
PERM_H = (128 * np.arange(44)[:, None] + _po[None, :]).ravel()

ROWPERM = np.empty(2 * I5, np.int64)
for c in range(44):
    ROWPERM[256 * c:256 * c + 128] = PERM_H[128 * c:128 * c + 128]
    ROWPERM[256 * c + 128:256 * c + 256] = I5 + PERM_H[128 * c:128 * c + 128]


def _byte_tiles(packed_rows, n_in):
    o = packed_rows.shape[0]
    b = packed_rows.view(np.uint8).reshape(o, n_in // 2).T
    nch = n_in // 128
    b = b.reshape(nch, 64, o)
    b = np.broadcast_to(b[:, None], (nch, 2, 64, o)).reshape(n_in, o)
    return np.ascontiguousarray(b)


def prep(inputs):
    x = np.asarray(inputs["L0_attn_x"])
    Wq = np.asarray(inputs["L0_attn_Wq"])
    Wk = np.asarray(inputs["L0_attn_Wk"])
    Wv = np.asarray(inputs["L0_attn_Wv"])
    Wo = np.asarray(inputs["L0_attn_Wo"])
    qs = np.asarray(inputs["L0_attn_q_scale"])
    ks = np.asarray(inputs["L0_attn_k_scale"])
    gp = np.asarray(inputs["L0_ffn_gate_up_packed"])
    gs = np.asarray(inputs["L0_ffn_gate_up_scales"])
    dp = np.asarray(inputs["L0_ffn_down_packed"])
    ds = np.asarray(inputs["L0_ffn_down_scales"])

    shared = {}
    shared["WqT"] = np.ascontiguousarray(Wq.T[PERM_D, :])
    shared["WkT"] = np.ascontiguousarray(Wk.T[PERM_D, :])
    shared["WvT"] = np.ascontiguousarray(Wv.T[PERM_D, :])
    shared["WoT"] = np.ascontiguousarray(Wo.T[:, PERM_D])
    shared["qs_row"] = np.ascontiguousarray(qs.reshape(1, HD))
    shared["ks_row"] = np.ascontiguousarray(ks.reshape(1, HD))

    gp2 = gp[ROWPERM]
    shared["guB"] = _byte_tiles(gp2, D)
    shared["guS"] = np.ascontiguousarray(np.repeat(gs[:, ROWPERM], 128, axis=0))

    dp2 = dp[PERM_D]
    shared["dnB"] = _byte_tiles(dp2, I5)
    shared["dnS"] = np.ascontiguousarray(np.repeat(ds[:, PERM_D], 128, axis=0))

    mask = np.zeros((128, 128), np.float32)
    iu = np.triu_indices(128, 1)
    mask[iu] = -1e9
    shared["masktile"] = mask
    shared["ident"] = np.eye(128, dtype=np.float32)

    per_core = [np.ascontiguousarray(x[b][:, PERM_D]) for b in range(x.shape[0])]
    return per_core, shared


def unprep_out(y_core):
    out = np.empty_like(y_core)
    out[:, PERM_D] = y_core
    return out


def kernel(L0_attn_x, L0_attn_Wq, L0_attn_Wk, L0_attn_Wv, L0_attn_Wo,
           L0_attn_q_scale, L0_attn_k_scale,
           L0_ffn_gate_up_packed, L0_ffn_gate_up_scales,
           L0_ffn_down_packed, L0_ffn_down_scales):
    from concourse.bass_utils import run_bass_kernel_spmd
    inputs = dict(
        L0_attn_x=L0_attn_x, L0_attn_Wq=L0_attn_Wq, L0_attn_Wk=L0_attn_Wk,
        L0_attn_Wv=L0_attn_Wv, L0_attn_Wo=L0_attn_Wo,
        L0_attn_q_scale=L0_attn_q_scale, L0_attn_k_scale=L0_attn_k_scale,
        L0_ffn_gate_up_packed=L0_ffn_gate_up_packed,
        L0_ffn_gate_up_scales=L0_ffn_gate_up_scales,
        L0_ffn_down_packed=L0_ffn_down_packed,
        L0_ffn_down_scales=L0_ffn_down_scales,
    )
    per_core, sh = prep(inputs)
    b = build(dump=(), stages="full")
    shared = dict(WqT=sh["WqT"], WkT=sh["WkT"], WvT=sh["WvT"], WoT=sh["WoT"],
                  qs_row=sh["qs_row"], ks_row=sh["ks_row"],
                  guB=sh["guB"], guS=sh["guS"], dnB=sh["dnB"], dnS=sh["dnS"],
                  maskt=sh["masktile"], ident=sh["ident"])
    in_maps = [dict(x=per_core[c], **shared) for c in range(8)]
    res = run_bass_kernel_spmd(b.nc, in_maps, core_ids=list(range(8)))
    out = np.empty((8, T, D), np.float32)
    for c in range(8):
        y = np.asarray(res.results[c]["yout"])
        out[c][:, PERM_D] = y
    return out


# version 24 (final)
# speedup vs baseline: 1.3231x; 1.3231x over previous
import math
import numpy as np
import ml_dtypes
import concourse.bass as bass
import concourse.mybir as mybir
from contextlib import ExitStack

F32 = mybir.dt.float32
BF16 = mybir.dt.bfloat16
F16 = mybir.dt.float16
U8 = mybir.dt.uint8
AF = mybir.ActivationFunctionType
ALU = mybir.AluOpType
AX = mybir.AxisListType

D = 2048
T = 2048
HD = 128
NH = 16
NKV = 4
I5 = 5632
NT = 4
TT = 512
DC = 16
HC = 44
INV_SQRT_HD = 1.0 / math.sqrt(HD)
CST = {4.0: 0, -2.0: 1, -6.0: 2, 1e-6: 3, 1e-5: 4}


class Ops:
    ENGS = ("sync", "tensor", "vector", "scalar", "gpsimd")

    def __init__(self, nc):
        self.nc = nc
        self.ops = {e: [] for e in self.ENGS}

    def emit(self, no_drain=True):
        with self.nc.Block(no_gpsimd_drain=no_drain) as blk:
            for e in self.ENGS:
                lst = self.ops[e]
                if lst:
                    getattr(blk, e)(lambda eng, _l=tuple(lst): [f(eng) for f in _l])


class Builder:
    def __init__(self, dump=()):
        self.dump = set(dump)
        self.nc = bass.Bass("TRN2", target_bir_lowering=False)
        nc = self.nc
        ap = lambda t: t.ap()
        self.x = ap(nc.dram_tensor("x", [T, D], F32, kind="ExternalInput"))
        self.WqT = ap(nc.dram_tensor("WqT", [D, 2048], F32, kind="ExternalInput"))
        self.WkT = ap(nc.dram_tensor("WkT", [D, 512], F32, kind="ExternalInput"))
        self.WvT = ap(nc.dram_tensor("WvT", [D, 512], F32, kind="ExternalInput"))
        self.WoT = ap(nc.dram_tensor("WoT", [2048, D], F32, kind="ExternalInput"))
        self.qs_row = ap(nc.dram_tensor("qs_row", [1, HD], F32, kind="ExternalInput"))
        self.ks_row = ap(nc.dram_tensor("ks_row", [1, HD], F32, kind="ExternalInput"))
        self.guB = ap(nc.dram_tensor("guB", [D, 2 * I5], U8, kind="ExternalInput"))
        self.guS = ap(nc.dram_tensor("guS", [D, 2 * I5], F16, kind="ExternalInput"))
        self.dnB = ap(nc.dram_tensor("dnB", [I5, D], U8, kind="ExternalInput"))
        self.dnS = ap(nc.dram_tensor("dnS", [I5, D], F16, kind="ExternalInput"))
        self.maskt = ap(nc.dram_tensor("maskt", [128, 128], F32, kind="ExternalInput"))
        self.ident = ap(nc.dram_tensor("ident", [128, 128], F32, kind="ExternalInput"))
        self.yout = ap(nc.dram_tensor("yout", [T, D], F32, kind="ExternalOutput"))
        k = lambda nm: "ExternalOutput" if nm in self.dump else "Internal"
        self.q_bf = ap(nc.dram_tensor("q_bf", [2048, T], BF16, kind=k("q_bf")))
        self.concat_bf = ap(nc.dram_tensor("concat_bf", [2048, T], BF16, kind=k("concat_bf")))
        self.wgu_bf = ap(nc.dram_tensor("wgu_bf", [D, 2 * I5], BF16, kind=k("wgu_bf")))
        self.wdn_bf = ap(nc.dram_tensor("wdn_bf", [I5, D], BF16, kind=k("wdn_bf")))
        self.wq16 = ap(nc.dram_tensor("wq16", [D, D], BF16, kind="Internal"))
        if "p4" in self.dump:
            self.d_x1 = ap(nc.dram_tensor("d_x1", [128, 4 * 2048], F32, kind="ExternalOutput"))
            self.d_ln2fm = ap(nc.dram_tensor("d_ln2fm", [128, DC * 512], BF16, kind="ExternalOutput"))
            self.d_h = ap(nc.dram_tensor("d_h", [128, HC * 512], BF16, kind="ExternalOutput"))
            self.d_y = ap(nc.dram_tensor("d_y", [128, 4 * 2048], F32, kind="ExternalOutput"))
            self.d_wdns = ap(nc.dram_tensor("d_wdns", [128, 4 * 512], BF16, kind="ExternalOutput"))
        if "p1" in self.dump:
            self.d_stats = ap(nc.dram_tensor("d_stats", [128, 24], F32, kind="ExternalOutput"))
            self.d_ln1fm = ap(nc.dram_tensor("d_ln1fm", [128, DC * TT], BF16, kind="ExternalOutput"))
            self.d_kfm = ap(nc.dram_tensor("d_kfm", [128, NKV * T], BF16, kind="ExternalOutput"))
            self.d_vtm = ap(nc.dram_tensor("d_vtm", [128, 16 * 512], BF16, kind="ExternalOutput"))
            self.d_rstdq = ap(nc.dram_tensor("d_rstdq", [128, 256], F32, kind="ExternalOutput"))
            self.d_mvec = ap(nc.dram_tensor("d_mvec", [128, 1], F32, kind="ExternalOutput"))

        self.top = ExitStack()
        self._sh = {}
        self._sn = {}
        sb = self.sb_top = lambda nm, shp, dt: self.top.enter_context(nc.sbuf_tensor(nm, shp, dt))
        self.ident16 = sb("ident16", [128, 128], BF16)
        self.mask_sb = sb("mask_sb", [128, 128], F32)
        self.mask16 = sb("mask16", [128, 128], BF16)
        self.mvec = sb("mvec", [128, 1], F32)
        self.rstdq = sb("rstdq", [128, 256], F32)
        self.cst = sb("cst", [128, 8], F32)
        self.trashA = sb("trashA", [128, 2048], F32)
        self.trashV = sb("trashV", [128, 2048], F32)
        self.attn_scope = ExitStack()
        asb = lambda nm, shp, dt: self.attn_scope.enter_context(nc.sbuf_tensor(nm, shp, dt))
        self.ident32 = asb("ident32", [128, 128], F32)
        self.dq_msk = asb("dq_msk", [128, 2], mybir.dt.int32)
        self.dq_csc = asb("dq_csc", [128, 1], F32)
        self.K_fm = asb("K_fm", [128, NKV * T], BF16)
        self.V_tm = asb("V_tm", [128, 16 * 512], BF16)
        FQ = self.FQ = 1024
        self.dq_b = [asb(f"dq_b{i}", [128, FQ], U8) for i in range(2)]
        self.dq_sc = [asb(f"dq_sc{i}", [128, FQ], F16) for i in range(2)]
        self.dq_r = [asb(f"dq_r{i}", [128, FQ], U8) for i in range(2)]
        self.dq_s = [asb(f"dq_s{i}", [128, FQ], U8) for i in range(2)]
        self.dq_rf = [asb(f"dq_rf{i}", [128, FQ], F32) for i in range(2)]
        self.dq_sg = [asb("dq_sg0", [128, FQ], F32)] * 2
        self.dq_r1 = [asb(f"dq_r1{i}", [128, FQ], F32) for i in range(2)]
        self.dq_r2 = [asb("dq_r20", [128, FQ], F32)] * 2
        self.dq_o = [asb(f"dq_o{i}", [128, FQ], BF16) for i in range(2)]
        self.dq_jobs = []
        for c in range(DC):
            for j in range(2 * I5 // FQ):
                self.dq_jobs.append((self.guB, self.guS, self.wgu_bf, c * 128, j * FQ))
        for c in range(HC):
            for j in range(D // FQ):
                self.dq_jobs.append((self.dnB, self.dnS, self.wdn_bf, c * 128, j * FQ))
        self.dq_i = 0
        self.dq_w_next = 0
        self.dq_gu_jobs = DC * (2 * I5 // FQ)

    def sem(self, name):
        if name not in self._sh:
            self._sh[name] = self.top.enter_context(self.nc.semaphore(name))
            self._sn[name] = 0
        return self._sh[name]

    def sval(self, name):
        self.sem(name)
        return self._sn[name]

    def op(self, B, eng, fn, inc=None, k=1):
        if inc is not None:
            h = self.sem(inc)
            self._sn[inc] += k
            v = self._sn[inc]

            def g(e, fn=fn, h=h, k=k):
                fn(e).then_inc(h, k)
            B.ops[eng].append(g)
            return v
        B.ops[eng].append(lambda e, fn=fn: fn(e))
        return None

    def opf(self, B, eng, fn):
        v = self.op(B, eng, fn, inc="ch_" + eng)
        self.wt(B, eng, "ch_" + eng, v)

    def wt(self, B, eng, name, val):
        if val is None or val <= 0:
            return
        h = self.sem(name)
        B.ops[eng].append(lambda e, h=h, v=val: e.wait_ge(h, v))

    def emit_dequant(self, B, n):
        FQ = self.FQ
        for _ in range(n):
            if self.dq_i >= len(self.dq_jobs):
                return
            j = self.dq_i
            self.dq_i += 1
            srcB, srcS, dst, r0, c0 = self.dq_jobs[j]
            bb, sc = self.dq_b[j % 2], self.dq_sc[j % 2]
            if j >= 2:
                self.wt(B, "sync", "dq_x", j - 1)
                self.wt(B, "sync", "dq_w", j - 1)
            self.op(B, "sync", lambda g, bb=bb, a=srcB[r0:r0 + 128, c0:c0 + FQ]:
                    g.dma_start(bb[:, :], a), inc="dq_gb", k=16)
            self.op(B, "sync", lambda g, sc=sc, a=srcS[r0:r0 + 128, c0:c0 + FQ]:
                    g.dma_start(sc[:, :], a), inc="dq_gs", k=16)
            if self.dq_w_next == j - 1:
                self._dq_w(B, j - 1)
            r, s = self.dq_r[j % 2], self.dq_s[j % 2]
            self.wt(B, "vector", "dq_gb", 16 * (j + 1))
            if j >= 2:
                self.wt(B, "vector", "dq_a", j - 1)
            self.op(B, "vector", lambda v, bb=bb, r=r: v.tensor_scalar(
                r[:, :], bb[:, :], self.dq_msk[:, 0:1], None, ALU.bitwise_and))
            self.op(B, "vector", lambda v, bb=bb, s=s: v.tensor_scalar(
                s[:, :], bb[:, :], self.dq_msk[:, 1:2], None, ALU.bitwise_and),
                    inc="dq_x")
            rf, sg, r1, r2 = self.dq_rf[j % 2], self.dq_sg[j % 2], self.dq_r1[j % 2], self.dq_r2[j % 2]
            self.wt(B, "scalar", "dq_x", j + 1)
            if j >= 1:
                self.wt(B, "scalar", "dq_p", j)
            if j >= 2:
                self.wt(B, "scalar", "dq_w", j - 1)
            self.op(B, "scalar", lambda a, rf=rf, r=r: a.activation(rf[:, :], r[:, :], AF.Copy, scale=self.dq_csc[:, :]))
            self.op(B, "scalar", lambda a, sg=sg, s=s: a.activation(sg[:, :], s[:, :], AF.Sign, bias=self.cs(4.0), scale=-1.0))
            self.op(B, "scalar", lambda a, r1=r1, rf=rf: a.activation(r1[:, :], rf[:, :], AF.Relu, bias=self.cs(-2.0), scale=1.0))
            self.op(B, "scalar", lambda a, r2=r2, rf=rf: a.activation(r2[:, :], rf[:, :], AF.Relu, bias=self.cs(-6.0), scale=2.0),
                    inc="dq_a")
            self.wt(B, "gpsimd", "dq_a", j + 1)
            self.op(B, "gpsimd", lambda g, r1=r1, r2=r2: g.tensor_tensor(r1[:, :], r1[:, :], r2[:, :], ALU.add))
            self.op(B, "gpsimd", lambda g, r1=r1, rf=rf: g.tensor_tensor(r1[:, :], r1[:, :], rf[:, :], ALU.add))
            self.op(B, "gpsimd", lambda g, r1=r1, sg=sg: g.tensor_tensor(r1[:, :], r1[:, :], sg[:, :], ALU.mult),
                    inc="dq_p")

    def _dq_w(self, B, j):
        FQ = self.FQ
        srcB, srcS, dst, r0, c0 = self.dq_jobs[j]
        sc, oo = self.dq_sc[j % 2], self.dq_o[j % 2]
        self.wt(B, "vector", "dq_p", j + 1)
        self.wt(B, "vector", "dq_gs", 16 * (j + 1))
        if j >= 2:
            self.wt(B, "vector", "dq_o", 16 * (j - 1))
        self.op(B, "vector", lambda v, sc=sc, oo=oo, r1=self.dq_r1[j % 2]: v.tensor_tensor(oo[:, :], r1[:, :], sc[:, :], ALU.mult),
                inc="dq_w")
        self.wt(B, "sync", "dq_w", j + 1)
        self.op(B, "sync", lambda g, oo=oo, a=dst[r0:r0 + 128, c0:c0 + FQ]: g.dma_start(a, oo[:, :]),
                inc="dq_o", k=16)
        self.dq_w_next = j + 1

    def dq_flush(self, B):
        if self.dq_w_next == self.dq_i - 1 and self.dq_i >= 1:
            self._dq_w(B, self.dq_i - 1)

    def cs(self, val):
        return self.cst[:, CST[val]:CST[val] + 1]

    def p0(self):
        B = Ops(self.nc)
        for val, i in CST.items():
            self.op(B, "vector", lambda v, val=val, i=i: v.memset(self.cst[:, i:i + 1], val))
        self.op(B, "vector", lambda v: v.memset(self.dq_msk[0:64, 0:1], 7))
        self.op(B, "vector", lambda v: v.memset(self.dq_msk[64:128, 0:1], 112))
        self.op(B, "vector", lambda v: v.memset(self.dq_msk[0:64, 1:2], 8))
        self.op(B, "vector", lambda v: v.memset(self.dq_msk[64:128, 1:2], 128))
        self.op(B, "vector", lambda v: v.memset(self.dq_csc[0:64, :], 0.5))
        self.op(B, "vector", lambda v: v.memset(self.dq_csc[64:128, :], 1.0 / 32.0))
        self.op(B, "gpsimd", lambda g: g.dma_start(self.ident16[:, :], self.ident), inc="dgp", k=16)
        self.op(B, "sync", lambda s: s.dma_start(self.ident32[:, :], self.ident), inc="dsy", k=16)
        self.op(B, "sync", lambda s: s.dma_start(self.mask_sb[:, :], self.maskt), inc="dsy", k=16)
        self.op(B, "gpsimd", lambda g: g.dma_start(self.mask16[:, :], self.maskt), inc="dgp", k=16)
        asb2 = lambda nm, shp, dt: self.attn_scope.enter_context(self.nc.sbuf_tensor(nm, shp, dt))
        qsr = self.qs32 = asb2("qs32", [128, 128], F32)
        ksr = self.ks32 = asb2("ks32", [128, 128], F32)
        self.op(B, "vector", lambda v: v.memset(self.rstdq[:, :], 0.0))
        self.op(B, "vector", lambda v: v.memset(qsr[:, :], 0.0))
        self.op(B, "vector", lambda v: v.memset(ksr[:, :], 0.0), inc="p0m")
        self.wt(B, "sync", "p0m", self.sval("p0m"))
        self.op(B, "sync", lambda s: s.dma_start(qsr[0:1, :], self.qs_row), inc="dsy", k=16)
        self.op(B, "sync", lambda s: s.dma_start(ksr[0:1, :], self.ks_row), inc="dsy", k=16)
        B.emit()
        B = Ops(self.nc)
        with self.nc.psum_tensor("ps_qk", [128, 512], F32) as ps_qk:
            self.wt(B, "tensor", "dsy", self.sval("dsy"))
            self.op(B, "tensor", lambda t: t.transpose(ps_qk[:, 0:128], self.qs32[:, :], self.ident32[:, :]))
            self.op(B, "tensor", lambda t: t.transpose(ps_qk[:, 128:256], self.ks32[:, :], self.ident32[:, :]),
                    inc="p0t")
            self.wt(B, "vector", "p0t", self.sval("p0t"))
            self.opf(B, "vector", lambda v: v.tensor_copy(self.trashV[:, 0:256], ps_qk[:, 0:256]))
            self.opf(B, "vector", lambda v: v.tensor_tensor(self.mvec[:, :], self.trashV[:, 0:1], self.trashV[:, 128:129], ALU.mult))
            self.op(B, "vector", lambda v: v.tensor_scalar(self.mvec[:, :], self.mvec[:, :], INV_SQRT_HD, None, ALU.mult))
            B.emit()

    def p1(self, parts="ab", ntiles=NT):
        nc = self.nc
        with ExitStack() as st:
            sb = lambda nm, shp, dt: st.enter_context(nc.sbuf_tensor(nm, shp, dt))
            bufs = dict(
                x_sb=sb("x_sb", [128, 2 * 2048], F32),
                ln1tm=sb("ln1tm", [128, 2 * 2048], BF16),
                ln1fm=sb("ln1fm", [128, DC * TT], BF16),
                rows=sb("rows", [128, 8], F32),
                m_sb=sb("m_sb", [128, 4], F32),
                t_a=sb("t_a", [128, 4], F32),
                t_b=sb("t_b", [128, 4], F32),
                rs_sb=sb("rs_sb", [128, 4], F32),
                wq_sb=[sb(f"wq_sb{i}", [128, DC * 512], BF16) for i in range(2)],
                wk_sb=sb("wk_sb", [128, DC * 512], BF16),
                wv_sb=sb("wv_sb", [128, DC * 512], BF16),
                ktm=sb("ktm", [128, 512], BF16),
                ksq=sb("ksq", [128, 8], F32),
                qtm=sb("qtm", [128, 4 * 512], BF16),
                qstage=[sb(f"qstage{i}", [128, 512], BF16) for i in range(2)],
            )
            for ti in range(ntiles):
                self._p1_tile(ti, bufs, parts)
            B = Ops(nc)
            self.op(B, "scalar", lambda a: a.activation(self.rstdq[:, :], self.rstdq[:, :], AF.Sqrt,
                                                        bias=self.cs(1e-6), scale=1.0 / HD), inc="p1f")
            self.wt(B, "vector", "p1f", self.sval("p1f"))
            self.op(B, "vector", lambda v: v.reciprocal(self.rstdq[:, :], self.rstdq[:, :]))
            if "p1" in self.dump:
                self.op(B, "sync", lambda s: s.dma_start(self.d_kfm, self.K_fm[:, :]), inc="dsy", k=16)
                self.op(B, "sync", lambda s: s.dma_start(self.d_vtm, self.V_tm[:, :]), inc="dsy", k=16)
            B.emit(no_drain=False)
            if "p1" in self.dump:
                B = Ops(nc)
                self.op(B, "sync", lambda s: s.dma_start(self.d_rstdq, self.rstdq[:, :]), inc="dsy", k=16)
                self.op(B, "sync", lambda s: s.dma_start(self.d_mvec, self.mvec[:, :]), inc="dsy", k=16)
                B.emit(no_drain=False)

    def _p1_tile(self, ti, bufs, parts="ab"):
        self._bufs_dbg = bufs
        nc = self.nc
        S = self.sval
        x_sb, ln1tm, ln1fm = bufs["x_sb"], bufs["ln1tm"], bufs["ln1fm"]
        rows, m_sb, t_a, t_b, rs_sb = bufs["rows"], bufs["m_sb"], bufs["t_a"], bufs["t_b"], bufs["rs_sb"]
        wq_sb, wk_sb, wv_sb = bufs["wq_sb"], bufs["wk_sb"], bufs["wv_sb"]
        ktm, ksq, qtm, qstage = bufs["ktm"], bufs["ksq"], bufs["qtm"], bufs["qstage"]

        def lnfm_dst(s, half):
            v = ln1fm[:, :].rearrange("p (dc f) -> p dc f", f=512)
            return v[:, half * 8:(half + 1) * 8, s * 128:(s + 1) * 128]

        B = Ops(nc)
        if ti == 0:
            self.op(B, "gpsimd", lambda g: g.dma_start(
                wk_sb[:, :], self.WkT.rearrange("(c p) f -> p c f", p=128)), inc="dgkv", k=16)
            self.op(B, "gpsimd", lambda g: g.dma_start(
                wv_sb[:, :], self.WvT.rearrange("(c p) f -> p c f", p=128)), inc="dgkv", k=16)
        for hg in range(2):
            if ti == 0:
                self.op(B, "gpsimd", lambda g, hg=hg: g.dma_start(
                    wq_sb[hg][:, :],
                    self.WqT[:, hg * 512:(hg + 1) * 512].rearrange("(c p) f -> p c f", p=128)),
                    inc="dgq", k=16)
            else:
                self.wt(B, "sync", "dc16", 16 * (hg + 1))
                self.op(B, "sync", lambda y, hg=hg: y.dma_start(
                    wq_sb[hg][:, :],
                    self.wq16[:, hg * 512:(hg + 1) * 512].rearrange("(c p) f -> p c f", p=128)),
                    inc="dgq", k=16)
        st_done = {}
        if ti > 0:
            self.wt(B, "vector", "p1st", self.sval("p1st"))
            self.wt(B, "scalar", "p1st", self.sval("p1st"))
        self.op(B, "vector", lambda v: v.memset(rows[:, 0:4], 0.0), inc="p1rm")
        self.op(B, "scalar", lambda a: a.memzero(rows[:, 4:8]), inc="p1rm")
        for s in range(4):
            if s == 0:
                self.wt(B, "vector", "p1rm", self.sval("p1rm"))
                self.wt(B, "scalar", "p1rm", self.sval("p1rm"))
            if s >= 2:
                self.wt(B, "sync", "p1st", st_done[s - 2])
            xd = self.op(B, "sync", lambda y, s=s: y.dma_start(
                x_sb[:, (s % 2) * 2048:(s % 2 + 1) * 2048],
                self.x[ti * TT + s * 128: ti * TT + (s + 1) * 128, :]), inc="dx", k=16)
            xsl = x_sb[:, (s % 2) * 2048:(s % 2 + 1) * 2048]
            self.wt(B, "vector", "dx", xd)
            self.op(B, "vector", lambda v, xsl=xsl, s=s: v.tensor_scalar(
                self.trashV[:, :], xsl, 1.0, 0.0, ALU.mult, ALU.add, accum_out=rows[:, s:s + 1]),
                inc="p1st")
            self.wt(B, "scalar", "dx", xd)
            self.op(B, "scalar", lambda a, xsl=xsl, s=s: a.activation(
                self.trashA[:, :], xsl, AF.Square, accum_out=rows[:, 4 + s:5 + s]),
                inc="p1st")
            st_done[s] = S("p1st")
        self.emit_dequant(B, 5)
        self.wt(B, "vector", "p1st", st_done[3])
        self.opf(B, "vector", lambda v: v.tensor_scalar(m_sb[:, :], rows[:, 0:4], 1.0 / D, None, ALU.mult))
        self.opf(B, "vector", lambda v: v.tensor_scalar(t_a[:, :], rows[:, 4:8], 1.0 / D, None, ALU.mult))
        self.opf(B, "vector", lambda v: v.tensor_tensor(t_b[:, :], m_sb[:, :], m_sb[:, :], ALU.mult))
        self.op(B, "vector", lambda v: v.tensor_tensor(t_a[:, :], t_a[:, :], t_b[:, :], ALU.subtract),
                inc="p1v")
        self.wt(B, "scalar", "p1v", S("p1v"))
        self.op(B, "scalar", lambda a: a.activation(t_b[:, :], t_a[:, :], AF.Sqrt, bias=self.cs(1e-5), scale=1.0),
                inc="p1sd")
        self.wt(B, "vector", "p1sd", S("p1sd"))
        self.opf(B, "vector", lambda v: v.reciprocal(rs_sb[:, :], t_b[:, :]))
        with nc.psum_tensor(f"ps_tr0_{ti}", [128, 1024], BF16) as pt0, \
             nc.psum_tensor(f"ps_tr1_{ti}", [128, 1024], BF16) as pt1:
            ps_tr = [pt0, pt1]
            ln_done = {}
            te_done = {}
            for s in range(4):
                if s < 2:
                    self.wt(B, "sync", "p1st", st_done[3])
                else:
                    self.wt(B, "sync", "p1ln", ln_done[s - 2])
                xd = self.op(B, "sync", lambda y, s=s: y.dma_start(
                    x_sb[:, (s % 2) * 2048:(s % 2 + 1) * 2048],
                    self.x[ti * TT + s * 128: ti * TT + (s + 1) * 128, :]), inc="dx", k=16)
                xsl = x_sb[:, (s % 2) * 2048:(s % 2 + 1) * 2048]
                self.wt(B, "vector", "dx", xd)
                if s >= 2:
                    self.wt(B, "vector", "p1tr", S("p1tr"))
                self.op(B, "vector", lambda v, xsl=xsl, s=s: v.tensor_scalar(
                    ln1tm[:, (s % 2) * 2048:(s % 2 + 1) * 2048], xsl,
                    m_sb[:, s:s + 1], rs_sb[:, s:s + 1], ALU.subtract, ALU.mult),
                    inc="p1ln")
                ln_done[s] = S("p1ln")
                for half in range(2):
                    u = 2 * s + half
                    self.wt(B, "tensor", "p1ln", ln_done[s])
                    if u >= 2:
                        self.wt(B, "tensor", "p1te", te_done[u - 2])
                    for i8 in range(8):
                        dc = half * 8 + i8
                        self.op(B, "tensor", lambda t, s=s, dc=dc, i8=i8, u=u: t.transpose(
                            ps_tr[u % 2][:, i8 * 128:(i8 + 1) * 128],
                            ln1tm[:, (s % 2) * 2048 + dc * 128: (s % 2) * 2048 + (dc + 1) * 128],
                            self.ident16[:, :]), inc="p1tr" if i8 == 7 else None)
                    self.wt(B, "vector", "p1tr", S("p1tr"))
                    self.op(B, "vector", lambda v, s=s, half=half, u=u: v.tensor_copy(
                        lnfm_dst(s, half),
                        ps_tr[u % 2][:, :].rearrange("p (dc f) -> p dc f", f=128)),
                        inc="p1te")
                    te_done[u] = S("p1te")
            B.emit()

        if not (set(parts) & set("bkq")):
            return
        do_kv = ("b" in parts) or ("k" in parts)
        do_q = ("b" in parts) or ("q" in parts)
        B = Ops(nc)
        self.emit_dequant(B, 7)
        if ti == 0 and "p1" in self.dump:
            self.op(B, "sync", lambda s: s.dma_start(self.d_ln1fm, ln1fm[:, :]), inc="dsy", k=16)
            self.op(B, "sync", lambda s: s.dma_start(self.d_stats[:, 0:8], rows[:, :]), inc="dsy", k=16)
            self.op(B, "sync", lambda s: s.dma_start(self.d_stats[:, 8:12], m_sb[:, :]), inc="dsy", k=16)
            self.op(B, "sync", lambda s: s.dma_start(self.d_stats[:, 12:16], rs_sb[:, :]), inc="dsy", k=16)
            self.op(B, "sync", lambda s: s.dma_start(self.d_stats[:, 16:20], bufs["t_a"][:, :]), inc="dsy", k=16)
            self.op(B, "sync", lambda s: s.dma_start(self.d_stats[:, 20:24], bufs["t_b"][:, :]), inc="dsy", k=16)
        with ExitStack() as st:
            psum = lambda nm: st.enter_context(nc.psum_tensor(f"{nm}_{ti}", [128, 512], F32))
            psum16 = lambda nm: st.enter_context(nc.psum_tensor(f"{nm}_{ti}", [128, 1024], BF16))
            ps_k = [psum("ps_k0"), psum("ps_k1")]
            ps_v = [psum("ps_v0"), psum("ps_v1")]
            ps_ktr = psum16("ps_ktr")
            ps_q = [psum("ps_q0"), psum("ps_q1")]
            ps_qtr = psum16("ps_qtr")
            if not do_kv:
                kv_iter = []
            else:
                kv_iter = list(range(4))
            if do_kv and ti == 0:
                self.wt(B, "tensor", "dgkv", S("dgkv"))
            kv_done = {}
            for tc in kv_iter:
                tcg = ti * 4 + tc
                if tc >= 2:
                    kn, ke = kv_done[tc - 2]
                    self.wt(B, "tensor", "p1kn", kn)
                    self.wt(B, "tensor", "p1ve", ke)
                for dc in range(DC):
                    lh = ln1fm[:, dc * 512 + tc * 128: dc * 512 + (tc + 1) * 128]
                    self.op(B, "tensor", lambda t, lh=lh, tc=tc, dc=dc: t.matmul(
                        ps_k[tc % 2][:, :], lh, wk_sb[:, dc * 512:(dc + 1) * 512],
                        start=(dc == 0), stop=(dc == DC - 1)))
                    self.op(B, "tensor", lambda t, lh=lh, tc=tc, dc=dc: t.matmul(
                        ps_v[tc % 2][:, :], lh, wv_sb[:, dc * 512:(dc + 1) * 512],
                        start=(dc == 0), stop=(dc == DC - 1)),
                        inc="p1kv" if dc == DC - 1 else None)
                self.wt(B, "vector", "p1kv", S("p1kv"))
                self.op(B, "vector", lambda v, tc=tc, tcg=tcg: v.tensor_copy(
                    self.V_tm[:, tcg * 512:(tcg + 1) * 512], ps_v[tc % 2][:, :]), inc="p1ve")
                self.wt(B, "scalar", "p1kv", S("p1kv"))
                self.op(B, "scalar", lambda a: a.memzero(ksq[:, 0:4]))
                for g in range(4):
                    self.op(B, "scalar", lambda a, tc=tc, g=g: a.activation(
                        self.trashA[:, 0:128], ps_k[tc % 2][:, g * 128:(g + 1) * 128],
                        AF.Square, accum_out=ksq[:, g:g + 1]))
                self.opf(B, "scalar", lambda a: a.nop())
                self.op(B, "scalar", lambda a: a.activation(
                    ksq[:, 4:8], ksq[:, 0:4], AF.Sqrt, bias=self.cs(1e-6), scale=1.0 / HD), inc="p1ks")
                self.wt(B, "vector", "p1ks", S("p1ks"))
                self.opf(B, "vector", lambda v: v.reciprocal(ksq[:, 4:8], ksq[:, 4:8]))
                for g in range(4):
                    self.op(B, "vector", lambda v, tc=tc, g=g: v.tensor_scalar(
                        ktm[:, g * 128:(g + 1) * 128], ps_k[tc % 2][:, g * 128:(g + 1) * 128],
                        ksq[:, 4 + g:5 + g], None, ALU.mult),
                        inc="p1kn" if g == 3 else None)
                self.wt(B, "tensor", "p1kn", S("p1kn"))
                if tc >= 1:
                    self.wt(B, "tensor", "p1kte", S("p1kte"))
                for g in range(4):
                    self.op(B, "tensor", lambda t, g=g: t.transpose(
                        ps_ktr[:, g * 128:(g + 1) * 128], ktm[:, g * 128:(g + 1) * 128],
                        self.ident16[:, :]), inc="p1kt" if g == 3 else None)
                self.wt(B, "vector", "p1kt", S("p1kt"))
                for g in range(4):
                    self.op(B, "vector", lambda v, g=g, tcg=tcg: v.tensor_copy(
                        self.K_fm[:, g * 2048 + tcg * 128: g * 2048 + (tcg + 1) * 128],
                        ps_ktr[:, g * 128:(g + 1) * 128]),
                        inc="p1kte" if g == 3 else None)
                kv_done[tc] = (S("p1kn"), S("p1ve"))
                self.emit_dequant(B, 2)
            if not do_q:
                B.emit()
                return
            qg_done = {}
            qt_done = {}
            qdma_done = {}
            self._q_mm_after = {}
            nhg = 4
            for ch in parts:
                if ch.isdigit():
                    nhg = int(ch)
            for hg in range(nhg):
                if hg >= 2:
                    if ti == 0:
                        self.wt(B, "gpsimd", "p1q_mm", self._q_mm_after[(hg - 2) * 4 + 3])
                        self.op(B, "gpsimd", lambda g, hg=hg: g.dma_start(
                            wq_sb[hg % 2][:, :],
                            self.WqT[:, hg * 512:(hg + 1) * 512].rearrange("(c p) f -> p c f", p=128)),
                            inc="dgq", k=16)
                        if hg == 3:
                            for cg in range(4):
                                self.op(B, "gpsimd", lambda g, cg=cg: g.dma_start(
                                    self.wq16[:, cg * 512:(cg + 1) * 512],
                                    self.WqT[:, cg * 512:(cg + 1) * 512]),
                                    inc="dc16", k=16)
                    else:
                        self.wt(B, "sync", "p1q_mm", self._q_mm_after[(hg - 2) * 4 + 3])
                        self.wt(B, "sync", "dc16", 16 * (hg + 1))
                        self.op(B, "sync", lambda y, hg=hg: y.dma_start(
                            wq_sb[hg % 2][:, :],
                            self.wq16[:, hg * 512:(hg + 1) * 512].rearrange("(c p) f -> p c f", p=128)),
                            inc="dgq", k=16)
                self.wt(B, "tensor", "dgq", 16 * (4 * ti + hg + 1))
                for s in range(4):
                    gq = hg * 4 + s
                    if gq >= 2:
                        ev, sq = qg_done[gq - 2]
                        self.wt(B, "tensor", "p1q_ev", ev)
                        self.wt(B, "tensor", "p1q_sq", sq)
                    if s == 0 and hg >= 1:
                        self.wt(B, "tensor", "p1qt", qt_done[hg - 1])
                    for dc in range(DC):
                        self.op(B, "tensor", lambda t, hg=hg, s=s, dc=dc: t.matmul(
                            ps_q[s % 2][:, :],
                            ln1fm[:, dc * 512 + s * 128: dc * 512 + (s + 1) * 128],
                            wq_sb[hg % 2][:, dc * 512:(dc + 1) * 512],
                            start=(dc == 0), stop=(dc == DC - 1)),
                            inc="p1q_mm" if dc == DC - 1 else None)
                    self._q_mm_after[gq] = S("p1q_mm")
                    self.wt(B, "vector", "p1q_mm", S("p1q_mm"))
                    self.op(B, "vector", lambda v, s=s: v.tensor_copy(
                        qtm[:, s * 512:(s + 1) * 512], ps_q[s % 2][:, :]), inc="p1q_ev")
                    if "z" not in parts:
                        self.wt(B, "scalar", "p1q_ev", S("p1q_ev"))
                        for hh in range(4):
                            col = (hg * 4 + hh) * 16 + ti * 4 + s
                            tgt = bufs["ksq"][:, 0:1] if "w" in parts else self.rstdq[:, col:col + 1]
                            if "u" in parts:
                                self.op(B, "scalar", lambda a, s=s, hh=hh: a.activation(
                                    self.trashA[:, 0:128], ps_q[s % 2][:, hh * 128:(hh + 1) * 128],
                                    AF.Square),
                                    inc="p1q_sq" if hh == 3 else None)
                            else:
                                self.op(B, "scalar", lambda a, s=s, hh=hh, tgt=tgt: a.activation(
                                    self.trashA[:, 0:128], ps_q[s % 2][:, hh * 128:(hh + 1) * 128],
                                    AF.Square, accum_out=tgt),
                                    inc="p1q_sq" if hh == 3 else None)
                    else:
                        self.op(B, "scalar", lambda a: a.nop(), inc="p1q_sq")
                    qg_done[gq] = (S("p1q_ev"), S("p1q_sq"))
                if "y" in parts:
                    qt_done[hg] = S("p1qt")
                    continue
                self.wt(B, "tensor", "p1q_ev", qg_done[hg * 4 + 3][0])
                for hh in range(4):
                    h = hg * 4 + hh
                    if h >= 1:
                        self.wt(B, "tensor", "p1qte", S("p1qte"))
                    for s in range(4):
                        self.op(B, "tensor", lambda t, hh=hh, s=s: t.transpose(
                            ps_qtr[:, s * 128:(s + 1) * 128],
                            qtm[:, s * 512 + hh * 128: s * 512 + (hh + 1) * 128],
                            self.ident16[:, :]), inc="p1qt" if s == 3 else None)
                    self.wt(B, "vector", "p1qt", S("p1qt"))
                    if h >= 2:
                        self.wt(B, "vector", "dqb", qdma_done[h - 2])
                    self.op(B, "vector", lambda v, h=h: v.tensor_scalar(
                        qstage[h % 2][:, :], ps_qtr[:, 0:512],
                        self.mvec[:, :], None, ALU.mult), inc="p1qte")
                    self.wt(B, "sync", "p1qte", S("p1qte"))
                    qdma_done[h] = self.op(B, "sync", lambda y, h=h: y.dma_start(
                        self.q_bf[h * 128:(h + 1) * 128, ti * TT:(ti + 1) * TT],
                        qstage[h % 2][:, :]), inc="dqb", k=16)
                qt_done[hg] = S("p1qt")
                self.emit_dequant(B, 2)
                self.emit_dequant(B, 2)
            B.emit()


    def p3(self):
        nc = self.nc
        S = self.sval
        with ExitStack() as st:
            sb = lambda nm, shp, dt: st.enter_context(nc.sbuf_tensor(nm, shp, dt))
            qh_sb = [sb(f"qh{i}", [128, 2048], BF16) for i in range(2)]
            scores = [sb(f"sc{i}", [128, 2048], F32) for i in range(2)]
            probs = [sb(f"pr{i}", [128, 2048], BF16) for i in range(4)]
            diag = [sb(f"dg{i}", [128, 128], BF16) for i in range(4)]
            pT = [sb(f"pT{i}", [128, 512], BF16) for i in range(2)]
            outh = [sb(f"oh{i}", [128, 512], BF16) for i in range(2)]
            nmx = sb("nmx", [128, 4], F32)
            nb = sb("nb", [128, 4], F32)
            rsum = sb("rsum", [128, 8], F32)
            ps_sc = [st.enter_context(nc.psum_tensor(f"ps_sc{i}", [128, 512], F32)) for i in range(2)]
            ps_pT = [st.enter_context(nc.psum_tensor(f"ps_pT{i}", [128, 512], F32)) for i in range(2)]
            ps_oh = [st.enter_context(nc.psum_tensor(f"ps_oh{i}", [128, 512], F32)) for i in range(2)]
            self._ps_oh = ps_oh
            self._pT = pT
            qh_mm_done = {}
            st_done = {}
            dmm_done = {}
            pte_done = {}
            oe_done = {}
            gidx = 0
            tglob = 0
            sqglob = 0
            B = Ops(nc)
            qh_dma = {}
            self.wt(B, "sync", "dqb", self.sval("dqb"))
            qh_dma[0] = self.op(B, "sync", lambda y: y.dma_start(
                qh_sb[0][:, :], self.q_bf[0:128, :]), inc="dqh", k=16)
            for h in range(NH):
                g = h // 4
                if h + 1 < NH:
                    if h >= 1:
                        self.wt(B, "sync", "p3mm", qh_mm_done[h - 1])
                    qh_dma[h + 1] = self.op(B, "sync", lambda y, h1=h + 1: y.dma_start(
                        qh_sb[h1 % 2][:, :], self.q_bf[h1 * 128:(h1 + 1) * 128, :]), inc="dqh", k=16)
                self.wt(B, "tensor", "dqh", qh_dma[h])
                for sq in range(4):
                    for qc4 in range(4):
                        qc = 4 * sq + qc4
                        kw = (qc + 1) * 128
                        ngrp = (kw + 511) // 512
                        for g0 in range(ngrp):
                            N = min(512, kw - g0 * 512)
                            last = (g0 == ngrp - 1)
                            if gidx >= 2:
                                self.wt(B, "tensor", "p3st", st_done[gidx - 2])
                            self.op(B, "tensor", lambda t, h=h, qc=qc, g=g, g0=g0, N=N, gi=gidx, last=last: t.matmul(
                                ps_sc[gi % 2][:, 0:N], qh_sb[h % 2][:, qc * 128:(qc + 1) * 128],
                                self.K_fm[:, g * 2048 + g0 * 512: g * 2048 + g0 * 512 + N],
                                start=True, stop=not last),
                                inc=None if last else "p3mm")
                            if last:
                                self.op(B, "tensor", lambda t, N=N, gi=gidx: t.matmul(
                                    ps_sc[gi % 2][:, N - 128:N], self.ident16[:, :],
                                    self.mask16[:, :], start=False, stop=True), inc="p3mm")
                            self.wt(B, "scalar", "p3mm", S("p3mm"))
                            self.op(B, "scalar", lambda a, qc=qc, g0=g0, N=N, gi=gidx: a.activation(
                                scores[qc % 2][:, g0 * 512:g0 * 512 + N], ps_sc[gi % 2][:, 0:N],
                                AF.Copy), inc="p3st")
                            st_done[gidx] = S("p3st")
                            gidx += 1
                        self.wt(B, "vector", "p3st", st_done[gidx - 1])
                        self.opf(B, "vector", lambda v, qc=qc, kw=kw, qc4=qc4: v.reduce_max(
                            nmx[:, qc4:qc4 + 1], scores[qc % 2][:, 0:kw], axis=AX.X, negate=True))
                        self.op(B, "vector", lambda v, h=h, qc=qc, qc4=qc4: v.tensor_tensor(
                            nb[:, qc4:qc4 + 1], nmx[:, qc4:qc4 + 1],
                            self.rstdq[:, h * 16 + qc:h * 16 + qc + 1], ALU.mult), inc="p3nb")
                        self.wt(B, "scalar", "p3nb", S("p3nb"))
                        if (sqglob - 1, qc4) in dmm_done:
                            self.wt(B, "scalar", "p3dmm", dmm_done[(sqglob - 1, qc4)])
                        self.op(B, "scalar", lambda a, h=h, qc=qc, qc4=qc4, kw=kw: a.activation(
                            probs[qc4][:, 0:kw], scores[qc % 2][:, 0:kw], AF.Exp,
                            bias=nb[:, qc4:qc4 + 1],
                            scale=self.rstdq[:, h * 16 + qc:h * 16 + qc + 1],
                            accum_out=rsum[:, qc4:qc4 + 1]), inc="p3ex")
                        self.wt(B, "vector", "p3ex", S("p3ex"))
                        self.opf(B, "vector", lambda v, qc4=qc4: v.reciprocal(
                            rsum[:, 4 + qc4:5 + qc4], rsum[:, qc4:qc4 + 1]))
                        self.op(B, "vector", lambda v, qc4=qc4: v.tensor_scalar(
                            diag[qc4][:, :], self.ident16[:, :], rsum[:, 4 + qc4:5 + qc4],
                            None, ALU.mult), inc="p3dg")
                    dg_now = S("p3dg")
                    self.emit_dequant(B, 2)
                    ntk = 4 * (sq + 1)
                    for tkc in range(ntk):
                        nz = [qc4 for qc4 in range(4) if tkc <= 4 * sq + qc4]
                        zr = [qc4 for qc4 in range(4) if tkc > 4 * sq + qc4]
                        if tglob >= 2:
                            self.wt(B, "vector", "p3dmm", dmm_done.get(("pt", tglob - 2), 0))
                        if zr:
                            for qc4 in zr:
                                self.op(B, "vector", lambda v, qc4=qc4, ti_=tglob: v.memset(
                                    ps_pT[ti_ % 2][:, qc4 * 128:(qc4 + 1) * 128], 0.0),
                                    inc="p3pz" if qc4 == zr[-1] else None)
                            self.wt(B, "tensor", "p3pz", S("p3pz"))
                        self.wt(B, "tensor", "p3dg", dg_now)
                        if tglob >= 2:
                            self.wt(B, "tensor", "p3pte", pte_done[tglob - 2])
                        for i, qc4 in enumerate(nz):
                            self.op(B, "tensor", lambda t, qc4=qc4, tkc=tkc, ti_=tglob: t.matmul(
                                ps_pT[ti_ % 2][:, qc4 * 128:(qc4 + 1) * 128],
                                probs[qc4][:, tkc * 128:(tkc + 1) * 128], diag[qc4][:, :],
                                start=True, stop=True),
                                inc="p3dmm" if i == len(nz) - 1 else None)
                        for qc4 in nz:
                            dmm_done[(sqglob, qc4)] = S("p3dmm")
                        dmm_done[("pt", tglob)] = S("p3dmm")
                        self.wt(B, "vector", "p3dmm", S("p3dmm"))
                        self.op(B, "vector", lambda v, ti_=tglob, tkc=tkc: v.tensor_copy(
                            pT[ti_ % 2][:, :], ps_pT[ti_ % 2][:, :]), inc="p3pte")
                        pte_done[tglob] = S("p3pte")
                        if tkc >= 1:
                            self._p3_av(B, g, sq, tkc - 1, tglob - 1, ntk, sqglob, pte_done)
                        if tkc == ntk - 1:
                            self._p3_av(B, g, sq, tkc, tglob, ntk, sqglob, pte_done)
                        tglob += 1
                    self.wt(B, "vector", "p3av", S("p3av"))
                    if sqglob >= 2:
                        self.wt(B, "vector", "dcn", oe_done.get(sqglob - 2, 0))
                    self._oh_evac = getattr(self, "_oh_evac", {})
                    self.op(B, "vector", lambda v, sq=sq: v.tensor_copy(
                        outh[sq % 2][:, :], ps_oh[sq % 2][:, :]), inc="p3oe")
                    self._oh_evac[sqglob] = S("p3oe")
                    self.wt(B, "sync", "p3oe", S("p3oe"))
                    oe_done[sqglob] = self.op(B, "sync", lambda y, h=h, sq=sq: y.dma_start(
                        self.concat_bf[h * 128:(h + 1) * 128, sq * 512:(sq + 1) * 512],
                        outh[sq % 2][:, :]), inc="dcn", k=16)
                    sqglob += 1
                qh_mm_done[h] = S("p3mm")
            B.emit()


    def _p3_av(self, B, g, sq, tkc, tg, ntk, sqglob, pte_done):
        S = self.sval
        self.wt(B, "tensor", "p3pte", pte_done[tg])
        if tkc == 0 and sqglob >= 2:
            self.wt(B, "tensor", "p3oe", self._oh_evac.get(sqglob - 2, 0))
        self.op(B, "tensor", lambda t, g=g, tkc=tkc, ti_=tg, last=(tkc == ntk - 1), sq=sq: t.matmul(
            self._ps_oh[sq % 2][:, :], self.V_tm[:, tkc * 512 + g * 128: tkc * 512 + (g + 1) * 128],
            self._pT[ti_ % 2][:, :], start=(tkc == 0), stop=last),
            inc="p3av" if tkc == ntk - 1 else None)

    def p4(self):
        nc = self.nc
        S = self.sval
        for ti in range(NT):
            with ExitStack() as tst:
                tsb = lambda nm, shp, dt: tst.enter_context(nc.sbuf_tensor(f"{nm}_{ti}", shp, dt))
                x1 = tsb("x1", [128, 4 * 2048], F32)
                ln2fm = tsb("ln2fm", [128, DC * 512], BF16)
                self._p4a(ti, tst, x1, ln2fm)
                self._p4b(ti, tst, x1, ln2fm)

    def _p4a(self, ti, tst, x1, ln2fm):
        nc = self.nc
        S = self.sval
        with ExitStack() as st:
            sb = lambda nm, shp, dt: st.enter_context(nc.sbuf_tensor(f"{nm}_{ti}", shp, dt))
            concat_sb = sb("cc", [128, 16 * 512], BF16)
            wo_sb = [sb(f"wo{i}", [128, 16 * 512], BF16) for i in range(2)]
            x_sb = sb("x4", [128, 4 * 2048], F32)
            ln2tm = sb("ln2tm", [128, 2 * 2048], BF16)
            rows = sb("rws", [128, 8], F32)
            m_sb = sb("m4", [128, 4], F32)
            t_a = sb("ta4", [128, 4], F32)
            t_b = sb("tb4", [128, 4], F32)
            rs_sb = sb("rs4", [128, 4], F32)
            ps_wo = [st.enter_context(nc.psum_tensor(f"ps_wo{i}_{ti}", [128, 512], F32)) for i in range(2)]
            B = Ops(nc)
            if ti == 0:
                self.emit_dequant(B, 24)
            if ti == 0:
                self.wt(B, "sync", "dcn", self.sval("dcn"))
            cd = self.op(B, "sync", lambda y: y.dma_start(
                concat_sb[:, :],
                self.concat_bf[:, ti * TT:(ti + 1) * TT].rearrange("(c p) t -> p c t", p=128)),
                inc="dcc", k=16)
            xds = []
            for s in range(4):
                xds.append(self.op(B, "sync", lambda y, s=s: y.dma_start(
                    x_sb[:, s * 2048:(s + 1) * 2048],
                    self.x[ti * TT + s * 128: ti * TT + (s + 1) * 128, :]), inc="dx2", k=16))
            godone = []
            for dr in range(2):
                godone.append(self.op(B, "gpsimd", lambda g, dr=dr: g.dma_start(
                    wo_sb[dr][:, :],
                    self.WoT[:, dr * 512:(dr + 1) * 512].rearrange("(c p) d -> p c d", p=128)),
                    inc="dgo", k=16))
            gidx = 0
            ev_done = {}
            for dr in range(4):
                if dr >= 2:
                    self.wt(B, "gpsimd", "p4wm", self._p4wm[dr - 2])
                    godone.append(self.op(B, "gpsimd", lambda g, dr=dr: g.dma_start(
                        wo_sb[dr % 2][:, :],
                        self.WoT[:, dr * 512:(dr + 1) * 512].rearrange("(c p) d -> p c d", p=128)),
                        inc="dgo", k=16))
                self.wt(B, "tensor", "dgo", godone[dr])
                if dr == 0:
                    self.wt(B, "tensor", "dcc", cd)
                for s in range(4):
                    if gidx >= 2:
                        self.wt(B, "tensor", "p4ev", ev_done[gidx - 2])
                    for cf in range(DC):
                        self.op(B, "tensor", lambda t, cf=cf, s=s, dr=dr, gi=gidx: t.matmul(
                            ps_wo[gi % 2][:, :],
                            concat_sb[:, cf * 512 + s * 128: cf * 512 + (s + 1) * 128],
                            wo_sb[dr % 2][:, cf * 512:(cf + 1) * 512],
                            start=(cf == 0), stop=(cf == DC - 1)),
                            inc="p4wm" if cf == DC - 1 else None)
                    self._p4wm = getattr(self, "_p4wm", {})
                    self._p4wm[dr] = S("p4wm")
                    self.wt(B, "vector", "p4wm", S("p4wm"))
                    self.wt(B, "vector", "dx2", xds[s])
                    self.op(B, "vector", lambda v, s=s, dr=dr, gi=gidx: v.tensor_tensor(
                        x1[:, s * 2048 + dr * 512: s * 2048 + (dr + 1) * 512],
                        ps_wo[gi % 2][:, :],
                        x_sb[:, s * 2048 + dr * 512: s * 2048 + (dr + 1) * 512], ALU.add),
                        inc="p4ev")
                    ev_done[gidx] = S("p4ev")
                    gidx += 1
            self.op(B, "vector", lambda v: v.memset(rows[:, 0:4], 0.0), inc="p4rm")
            self.op(B, "scalar", lambda a: a.memzero(rows[:, 4:8]), inc="p4rm")
            self.wt(B, "vector", "p4ev", S("p4ev"))
            self.wt(B, "scalar", "p4ev", S("p4ev"))
            self.wt(B, "scalar", "p4rm", S("p4rm"))
            self.wt(B, "vector", "p4rm", S("p4rm"))
            for s in range(4):
                xsl = x1[:, s * 2048:(s + 1) * 2048]
                self.op(B, "vector", lambda v, xsl=xsl, s=s: v.tensor_scalar(
                    self.trashV[:, :], xsl, 1.0, 0.0, ALU.mult, ALU.add, accum_out=rows[:, s:s + 1]),
                    inc="p4st")
                self.op(B, "scalar", lambda a, xsl=xsl, s=s: a.activation(
                    self.trashA[:, :], xsl, AF.Square, accum_out=rows[:, 4 + s:5 + s]),
                    inc="p4st")
            self.wt(B, "vector", "p4st", S("p4st"))
            self.opf(B, "vector", lambda v: v.tensor_scalar(m_sb[:, :], rows[:, 0:4], 1.0 / D, None, ALU.mult))
            self.opf(B, "vector", lambda v: v.tensor_scalar(t_a[:, :], rows[:, 4:8], 1.0 / D, None, ALU.mult))
            self.opf(B, "vector", lambda v: v.tensor_tensor(t_b[:, :], m_sb[:, :], m_sb[:, :], ALU.mult))
            self.op(B, "vector", lambda v: v.tensor_tensor(t_a[:, :], t_a[:, :], t_b[:, :], ALU.subtract),
                    inc="p4v")
            self.wt(B, "scalar", "p4v", S("p4v"))
            self.op(B, "scalar", lambda a: a.activation(t_b[:, :], t_a[:, :], AF.Sqrt, bias=self.cs(1e-5), scale=1.0),
                    inc="p4sd")
            self.wt(B, "vector", "p4sd", S("p4sd"))
            self.opf(B, "vector", lambda v: v.reciprocal(rs_sb[:, :], t_b[:, :]))
            with nc.psum_tensor(f"ps_tr4a_{ti}", [128, 1024], BF16) as pt0, \
                 nc.psum_tensor(f"ps_tr4b_{ti}", [128, 1024], BF16) as pt1:
                ps_tr = [pt0, pt1]
                te_done = {}
                for s in range(4):
                    if s >= 2:
                        self.wt(B, "vector", "p4tr", S("p4tr"))
                    self.op(B, "vector", lambda v, s=s: v.tensor_scalar(
                        ln2tm[:, (s % 2) * 2048:(s % 2 + 1) * 2048], x1[:, s * 2048:(s + 1) * 2048],
                        m_sb[:, s:s + 1], rs_sb[:, s:s + 1], ALU.subtract, ALU.mult),
                        inc="p4ln")
                    lnd = S("p4ln")
                    for half in range(2):
                        u = 2 * s + half
                        self.wt(B, "tensor", "p4ln", lnd)
                        if u >= 2:
                            self.wt(B, "tensor", "p4te", te_done[u - 2])
                        for i8 in range(8):
                            dc = half * 8 + i8
                            self.op(B, "tensor", lambda t, s=s, dc=dc, i8=i8, u=u: t.transpose(
                                ps_tr[u % 2][:, i8 * 128:(i8 + 1) * 128],
                                ln2tm[:, (s % 2) * 2048 + dc * 128: (s % 2) * 2048 + (dc + 1) * 128],
                                self.ident16[:, :]), inc="p4trr" if i8 == 7 else None)
                        self.wt(B, "vector", "p4trr", S("p4trr"))
                        fmv = ln2fm[:, :].rearrange("p (dc f) -> p dc f", f=512)
                        dst = fmv[:, half * 8:(half + 1) * 8, s * 128:(s + 1) * 128]
                        self.op(B, "vector", lambda v, dst=dst, u=u: v.tensor_copy(
                            dst, ps_tr[u % 2][:, :].rearrange("p (dc f) -> p dc f", f=128)),
                            inc="p4te")
                        te_done[u] = S("p4te")
                        self.op(B, "vector", lambda v: v.nop(), inc="p4tr")
                if ti == 0 and "p4" in self.dump:
                    self.wt(B, "sync", "p4te", S("p4te"))
                    self.op(B, "sync", lambda y: y.dma_start(self.d_x1, x1[:, :]), inc="dsy", k=16)
                    self.op(B, "sync", lambda y: y.dma_start(self.d_ln2fm, ln2fm[:, :]), inc="dsy", k=16)
                B.emit()

    def _p4b(self, ti, tst, x1, ln2fm):
        nc = self.nc
        S = self.sval
        with ExitStack() as st:
            sb = lambda nm, shp, dt: st.enter_context(nc.sbuf_tensor(f"{nm}_{ti}", shp, dt))
            wgu_sb = [sb(f"wgu{i}", [128, 16 * 512], BF16) for i in range(2)]
            wdn_sb = [sb(f"wdn{i}", [128, 4 * 512], BF16) for i in range(2)]
            h_sb = sb("hh", [128, HC * 512], BF16)
            y_sb = sb("yy", [128, 4 * 2048], F32)
            silu_sb = [sb(f"sl{i}", [128, 512], F32) for i in range(2)]
            out_sb = [sb(f"ou{i}", [128, 2048], F32) for i in range(2)]
            ysq = sb("ysq", [128, 24], F32)
            ps_g = [st.enter_context(nc.psum_tensor(f"ps_g{i}_{ti}", [128, 512], F32)) for i in range(2)]
            ps_u = [st.enter_context(nc.psum_tensor(f"ps_u{i}_{ti}", [128, 512], F32)) for i in range(2)]
            ps_y = [st.enter_context(nc.psum_tensor(f"ps_y{i}_{ti}", [128, 512], F32)) for i in range(4)]
            B = Ops(nc)
            if ti == 0:
                self.wt(B, "sync", "dq_o", 16 * self.dq_gu_jobs)
            gu_done = []
            for orange in range(2):
                self.op(B, "sync", lambda y, orange=orange: y.dma_start(
                    wgu_sb[orange][:, 0:4096],
                    self.wgu_bf[0:1024, orange * 512:(orange + 1) * 512].rearrange("(c p) o -> p c o", p=128)),
                    inc="dgu", k=16)
                gu_done.append(self.op(B, "sync", lambda y, orange=orange: y.dma_start(
                    wgu_sb[orange][:, 4096:8192],
                    self.wgu_bf[1024:2048, orange * 512:(orange + 1) * 512].rearrange("(c p) o -> p c o", p=128)),
                    inc="dgu", k=16))
            sl_done = {}
            h_done = {}
            for orange in range(22):
                if orange >= 2:
                    self.wt(B, "sync", "p4gm", self._p4gm[(orange - 2) * 2 + 1])
                    self.op(B, "sync", lambda y, orange=orange: y.dma_start(
                        wgu_sb[orange % 2][:, 0:4096],
                        self.wgu_bf[0:1024, orange * 512:(orange + 1) * 512].rearrange("(c p) o -> p c o", p=128)),
                        inc="dgu", k=16)
                    gu_done.append(self.op(B, "sync", lambda y, orange=orange: y.dma_start(
                        wgu_sb[orange % 2][:, 4096:8192],
                        self.wgu_bf[1024:2048, orange * 512:(orange + 1) * 512].rearrange("(c p) o -> p c o", p=128)),
                        inc="dgu", k=16))
                self.wt(B, "tensor", "dgu", gu_done[orange])
                for pidx in range(2):
                    pp = orange * 2 + pidx
                    j0 = 2 * pidx
                    if pp >= 2:
                        self.wt(B, "tensor", "p4sl", sl_done[pp - 2])
                        self.wt(B, "tensor", "p4h", h_done[pp - 2])
                    for i in range(DC):
                        self.op(B, "tensor", lambda t, i=i, j0=j0, pp=pp, orange=orange: t.matmul(
                            ps_g[pp % 2][:, :],
                            wgu_sb[orange % 2][:, i * 512 + j0 * 128: i * 512 + (j0 + 1) * 128],
                            ln2fm[:, i * 512:(i + 1) * 512],
                            start=(i == 0), stop=(i == DC - 1)))
                        self.op(B, "tensor", lambda t, i=i, j0=j0, pp=pp, orange=orange: t.matmul(
                            ps_u[pp % 2][:, :],
                            wgu_sb[orange % 2][:, i * 512 + (j0 + 1) * 128: i * 512 + (j0 + 2) * 128],
                            ln2fm[:, i * 512:(i + 1) * 512],
                            start=(i == 0), stop=(i == DC - 1)),
                            inc="p4gm" if i == DC - 1 else None)
                    self._p4gm = getattr(self, "_p4gm", {})
                    self._p4gm[pp] = S("p4gm")
                    self.wt(B, "scalar", "p4gm", S("p4gm"))
                    if pp >= 2:
                        self.wt(B, "scalar", "p4h", h_done[pp - 2])
                    self.op(B, "scalar", lambda a, pp=pp: a.activation(
                        silu_sb[pp % 2][:, :], ps_g[pp % 2][:, :], AF.Silu), inc="p4sl")
                    sl_done[pp] = S("p4sl")
                    self.wt(B, "vector", "p4sl", sl_done[pp])
                    self.op(B, "vector", lambda v, pp=pp: v.tensor_tensor(
                        h_sb[:, pp * 512:(pp + 1) * 512], silu_sb[pp % 2][:, :],
                        ps_u[pp % 2][:, :], ALU.mult), inc="p4h")
                    h_done[pp] = S("p4h")
            if ti == 0:
                self.wt(B, "sync", "dq_o", 16 * len(self.dq_jobs))
            dn_done = {}
            ye_done = {}
            didx = 0
            for dr in range(4):
                for iq in range(11):
                    if didx >= 2:
                        self.wt(B, "sync", "p4dm", dn_done.get(didx - 2, 0))
                    self.op(B, "sync", lambda y, dr=dr, iq=iq, di=didx: y.dma_start(
                        wdn_sb[di % 2][:, 0:1024],
                        self.wdn_bf[iq * 512:iq * 512 + 256, dr * 512:(dr + 1) * 512]
                            .rearrange("(c p) d -> p c d", p=128)),
                        inc="dgd", k=16)
                    dnd = self.op(B, "sync", lambda y, dr=dr, iq=iq, di=didx: y.dma_start(
                        wdn_sb[di % 2][:, 1024:2048],
                        self.wdn_bf[iq * 512 + 256:(iq + 1) * 512, dr * 512:(dr + 1) * 512]
                            .rearrange("(c p) d -> p c d", p=128)),
                        inc="dgd", k=16)
                    self.wt(B, "tensor", "dgd", dnd)
                    if ti == 0 and dr == 0 and iq == 0 and "p4" in self.dump:
                        self.wt(B, "sync", "dgd", dnd)
                        self.op(B, "sync", lambda y: y.dma_start(self.d_wdns, wdn_sb[0][:, :]), inc="dsy", k=16)
                    self.wt(B, "tensor", "p4h", h_done[iq * 4 + 3])
                    if iq == 0:
                        for tsub in range(4):
                            if dr >= 1:
                                self.wt(B, "tensor", "p4ye", ye_done[(dr - 1, tsub)])
                    for i4 in range(4):
                        ip = iq * 4 + i4
                        for tsub in range(4):
                            self.op(B, "tensor", lambda t, ip=ip, i4=i4, tsub=tsub, di=didx: t.matmul(
                                ps_y[tsub][:, :],
                                h_sb[:, ip * 512 + tsub * 128: ip * 512 + (tsub + 1) * 128],
                                wdn_sb[di % 2][:, i4 * 512:(i4 + 1) * 512],
                                start=(ip == 0), stop=(ip == HC - 1)),
                                inc="p4dm" if (i4 == 3 and tsub == 3) else None)
                    dn_done[didx] = S("p4dm")
                    didx += 1
                self.wt(B, "vector", "p4dm", S("p4dm"))
                for tsub in range(4):
                    self.op(B, "vector", lambda v, tsub=tsub, dr=dr: v.tensor_copy(
                        y_sb[:, tsub * 2048 + dr * 512: tsub * 2048 + (dr + 1) * 512],
                        ps_y[tsub][:, :]), inc="p4ye")
                    ye_done[(dr, tsub)] = S("p4ye")
                    self.wt(B, "scalar", "p4ye", S("p4ye"))
                    self.op(B, "scalar", lambda a, tsub=tsub, dr=dr: a.activation(
                        self.trashA[:, 0:512],
                        y_sb[:, tsub * 2048 + dr * 512: tsub * 2048 + (dr + 1) * 512],
                        AF.Square, accum_out=ysq[:, tsub * 4 + dr:tsub * 4 + dr + 1]),
                        inc="p4ys")
            self.wt(B, "vector", "p4ys", S("p4ys"))
            for tsub in range(4):
                self.opf(B, "vector", lambda v, tsub=tsub: v.tensor_reduce(
                    ysq[:, 16 + tsub:17 + tsub], ysq[:, tsub * 4:tsub * 4 + 4],
                    AX.X, ALU.add))
            self.op(B, "vector", lambda v: v.nop(), inc="p4yr")
            self.wt(B, "scalar", "p4yr", S("p4yr"))
            self.op(B, "scalar", lambda a: a.activation(
                ysq[:, 20:24], ysq[:, 16:20], AF.Sqrt, bias=self.cs(1e-6), scale=1.0 / D),
                inc="p4rm2")
            self.wt(B, "vector", "p4rm2", S("p4rm2"))
            self.opf(B, "vector", lambda v: v.reciprocal(ysq[:, 20:24], ysq[:, 20:24]))
            od_done = {}
            for tsub in range(4):
                if tsub >= 2:
                    self.wt(B, "vector", "dout", od_done[tsub - 2])
                self.op(B, "vector", lambda v, tsub=tsub: v.scalar_tensor_tensor(
                    out_sb[tsub % 2][:, :], y_sb[:, tsub * 2048:(tsub + 1) * 2048],
                    ysq[:, 20 + tsub:21 + tsub], x1[:, tsub * 2048:(tsub + 1) * 2048],
                    ALU.mult, ALU.add), inc="p4o")
                self.wt(B, "sync", "p4o", S("p4o"))
                od_done[tsub] = self.op(B, "sync", lambda y, tsub=tsub: y.dma_start(
                    self.yout[ti * TT + tsub * 128: ti * TT + (tsub + 1) * 128, :],
                    out_sb[tsub % 2][:, :]), inc="dout", k=16)
            if ti == 0 and "p4" in self.dump:
                self.op(B, "sync", lambda y: y.dma_start(self.d_h, h_sb[:, :]), inc="dsy", k=16)
                self.op(B, "sync", lambda y: y.dma_start(self.d_y, y_sb[:, :]), inc="dsy", k=16)
            B.emit()


def build(dump=(), stages="full", ndq=None):
    b = Builder(dump)
    if ndq is not None:
        b.dq_jobs = b.dq_jobs[:ndq]
    b.p0()
    if stages in ("p1", "full"):
        b.p1()
    if stages in ("p3", "full"):
        if stages == "p3":
            b.p1()
        b.p3()
    if stages == "full":
        B = Ops(b.nc)
        while b.dq_i < len(b.dq_jobs):
            b.emit_dequant(B, 24)
        b.dq_flush(B)
        B.emit()
    b.attn_scope.close()
    if stages == "full":
        b.p4()
    elif stages.startswith("p1:"):
        _, parts, nt = stages.split(":")
        b.p1(parts, int(nt))
    if stages in ("dq", "full"):
        while b.dq_i < len(b.dq_jobs):
            B = Ops(b.nc)
            b.emit_dequant(B, 24)
            b.dq_flush(B)
            B.emit()
    B = Ops(b.nc)
    b.dq_flush(B)
    if b.dq_i:
        b.wt(B, "sync", "dq_o", 16 * b.dq_i)
    for s_ in ("dsy", "dx", "dqb", "dq_gb", "dq_gs", "dqh", "dcn",
               "dcc", "dx2", "dgo", "dgu", "dgd", "dout"):
        b.wt(B, "sync", s_, b.sval(s_))
    B.emit(no_drain=False)
    b.top.close()
    return b


_po = np.empty(128, np.int64)
_po[:64] = 2 * np.arange(64)
_po[64:] = 2 * np.arange(64) + 1
PERM_D = (128 * np.arange(16)[:, None] + _po[None, :]).ravel()
PERM_H = (128 * np.arange(44)[:, None] + _po[None, :]).ravel()

ROWPERM = np.empty(2 * I5, np.int64)
for c in range(44):
    ROWPERM[256 * c:256 * c + 128] = PERM_H[128 * c:128 * c + 128]
    ROWPERM[256 * c + 128:256 * c + 256] = I5 + PERM_H[128 * c:128 * c + 128]


def _byte_tiles(packed_rows, n_in):
    o = packed_rows.shape[0]
    b = packed_rows.view(np.uint8).reshape(o, n_in // 2).T
    nch = n_in // 128
    b = b.reshape(nch, 64, o)
    b = np.broadcast_to(b[:, None], (nch, 2, 64, o)).reshape(n_in, o)
    return np.ascontiguousarray(b)


def prep(inputs):
    x = np.asarray(inputs["L0_attn_x"])
    Wq = np.asarray(inputs["L0_attn_Wq"])
    Wk = np.asarray(inputs["L0_attn_Wk"])
    Wv = np.asarray(inputs["L0_attn_Wv"])
    Wo = np.asarray(inputs["L0_attn_Wo"])
    qs = np.asarray(inputs["L0_attn_q_scale"])
    ks = np.asarray(inputs["L0_attn_k_scale"])
    gp = np.asarray(inputs["L0_ffn_gate_up_packed"])
    gs = np.asarray(inputs["L0_ffn_gate_up_scales"])
    dp = np.asarray(inputs["L0_ffn_down_packed"])
    ds = np.asarray(inputs["L0_ffn_down_scales"])

    shared = {}
    shared["WqT"] = np.ascontiguousarray(Wq.T[PERM_D, :])
    shared["WkT"] = np.ascontiguousarray(Wk.T[PERM_D, :])
    shared["WvT"] = np.ascontiguousarray(Wv.T[PERM_D, :])
    shared["WoT"] = np.ascontiguousarray(Wo.T[:, PERM_D])
    shared["qs_row"] = np.ascontiguousarray(qs.reshape(1, HD))
    shared["ks_row"] = np.ascontiguousarray(ks.reshape(1, HD))

    gp2 = gp[ROWPERM]
    shared["guB"] = _byte_tiles(gp2, D)
    shared["guS"] = np.ascontiguousarray(np.repeat(gs[:, ROWPERM], 128, axis=0))

    dp2 = dp[PERM_D]
    shared["dnB"] = _byte_tiles(dp2, I5)
    shared["dnS"] = np.ascontiguousarray(np.repeat(ds[:, PERM_D], 128, axis=0))

    mask = np.zeros((128, 128), np.float32)
    iu = np.triu_indices(128, 1)
    mask[iu] = -1e9
    shared["masktile"] = mask
    shared["ident"] = np.eye(128, dtype=np.float32)

    per_core = [np.ascontiguousarray(x[b][:, PERM_D]) for b in range(x.shape[0])]
    return per_core, shared


def unprep_out(y_core):
    out = np.empty_like(y_core)
    out[:, PERM_D] = y_core
    return out


def kernel(L0_attn_x, L0_attn_Wq, L0_attn_Wk, L0_attn_Wv, L0_attn_Wo,
           L0_attn_q_scale, L0_attn_k_scale,
           L0_ffn_gate_up_packed, L0_ffn_gate_up_scales,
           L0_ffn_down_packed, L0_ffn_down_scales):
    from concourse.bass_utils import run_bass_kernel_spmd
    inputs = dict(
        L0_attn_x=L0_attn_x, L0_attn_Wq=L0_attn_Wq, L0_attn_Wk=L0_attn_Wk,
        L0_attn_Wv=L0_attn_Wv, L0_attn_Wo=L0_attn_Wo,
        L0_attn_q_scale=L0_attn_q_scale, L0_attn_k_scale=L0_attn_k_scale,
        L0_ffn_gate_up_packed=L0_ffn_gate_up_packed,
        L0_ffn_gate_up_scales=L0_ffn_gate_up_scales,
        L0_ffn_down_packed=L0_ffn_down_packed,
        L0_ffn_down_scales=L0_ffn_down_scales,
    )
    per_core, sh = prep(inputs)
    b = build(dump=(), stages="full")
    shared = dict(WqT=sh["WqT"], WkT=sh["WkT"], WvT=sh["WvT"], WoT=sh["WoT"],
                  qs_row=sh["qs_row"], ks_row=sh["ks_row"],
                  guB=sh["guB"], guS=sh["guS"], dnB=sh["dnB"], dnS=sh["dnS"],
                  maskt=sh["masktile"], ident=sh["ident"])
    in_maps = [dict(x=per_core[c], **shared) for c in range(8)]
    res = run_bass_kernel_spmd(b.nc, in_maps, core_ids=list(range(8)))
    out = np.empty((8, T, D), np.float32)
    for c in range(8):
        y = np.asarray(res.results[c]["yout"])
        out[c][:, PERM_D] = y
    return out
